# version 06
# baseline (speedup 1.0000x reference)
import numpy as np
import ml_dtypes
from contextlib import ExitStack
import concourse.bass as bass
import concourse.mybir as mybir
from concourse.bass_utils import run_bass_kernel_spmd

F32 = mybir.dt.float32
BF16 = mybir.dt.bfloat16
AF = mybir.ActivationFunctionType
ALU = mybir.AluOpType
BF = ml_dtypes.bfloat16

LN_EPS = 1e-5


class Cfg:
    def __init__(self, D=1024, T=2048, DH=4096):
        self.D = D
        self.T = T
        self.S = 2 * T
        self.DK = 64
        self.H = D // 64
        self.DH = DH
        self.DC = D // 128
        self.HC = DH // 128
        self.NT = T // 512
        self.NTS = 2 * T // 512


class P:

    ENGS = ("te", "ve", "sc", "gp", "sy")

    def __init__(self, nc, ctx):
        self.nc = nc
        self.ctx = ctx
        self._sems = {}
        self.cnt = {}
        self.ops = {e: [] for e in self.ENGS}

    def sem(self, name):
        if name not in self._sems:
            self._sems[name] = self.ctx.enter_context(self.nc.semaphore(name))
            self.cnt[name] = 0
        return self._sems[name]

    def W(self, eng, semname, val=None):
        s = self.sem(semname)
        v = self.cnt[semname] if val is None else val
        if v > 0:
            self.ops[eng].append(lambda E, s=s, v=v: E.wait_ge(s, v))

    def OP(self, eng, fn, sem=None, n=1):
        if eng in ("ve", "sc"):
            s = self.sem(eng)
            v = self.cnt[eng]
            if v > 0:
                self.ops[eng].append(lambda E, s=s, v=v: E.wait_ge(s, v))
            self.cnt[eng] += 1

            def run(E, fn=fn, s=s):
                fn(E).then_inc(s, 1)
            self.ops[eng].append(run)
            return self.cnt[eng]
        if sem is not None:
            s = self.sem(sem)
            self.cnt[sem] += n

            def run2(E, fn=fn, s=s, n=n):
                fn(E).then_inc(s, n)
            self.ops[eng].append(run2)
        else:
            self.ops[eng].append(lambda E, fn=fn: fn(E))
        return self.cnt.get(sem, 0)

    def DMA(self, eng, out, in_, sem):
        return self.OP(eng, lambda E, o=out, i=in_: E.dma_start(out=o, in_=i),
                       sem=sem, n=16)

    def emit(self, block):
        @block.tensor
        def _(E):
            for f in self.ops["te"]:
                f(E)

        @block.vector
        def _(E):
            for f in self.ops["ve"]:
                f(E)

        @block.scalar
        def _(E):
            for f in self.ops["sc"]:
                f(E)

        @block.gpsimd
        def _(E):
            for f in self.ops["gp"]:
                f(E)

        @block.sync
        def _(E):
            for f in self.ops["sy"]:
                f(E)


def emit_ln(p, cfg, W, src_bf, out_cb, st0, st1, rb, mb, ONES, R0, R1, R2r,
            RB2, SQ_of):
    DC, D = cfg.DC, cfg.D
    p.W("sc", "ve")
    p.W("sc", "te")
    for c in range(DC):
        p.OP("sc", lambda E, c=c: E.activation(SQ_of(c), src_bf(c), AF.Square),
             sem="sc" if c == DC - 1 else None)
    p.W("te", "sc")
    for c in range(DC):
        p.OP("te", lambda E, c=c: E.matmul(st0, ONES[:, 0:1], src_bf(c),
                                           start=(c == 0), stop=(c == DC - 1)))
    for c in range(DC):
        p.OP("te", lambda E, c=c: E.matmul(st1, ONES[:, 0:1], SQ_of(c),
                                           start=(c == 0), stop=(c == DC - 1)),
             sem="te" if c == DC - 1 else None)
    p.W("ve", "te")
    p.OP("ve", lambda E: E.tensor_scalar_mul(R0, st0, 1.0 / D))
    p.OP("ve", lambda E: E.tensor_scalar_mul(R1, st1, 1.0 / D))
    p.OP("ve", lambda E: E.tensor_tensor(R2r, R0, R0, ALU.mult))
    p.OP("ve", lambda E: E.tensor_tensor(R1, R1, R2r, ALU.subtract))
    p.OP("ve", lambda E: E.tensor_scalar_add(R1, R1, LN_EPS))
    p.OP("ve", lambda E: E.reciprocal(R1, R1), sem="ve")
    p.W("sc", "ve")
    p.OP("sc", lambda E: E.activation(R1, R1, AF.Sqrt), sem="sc")
    p.W("ve", "sc")
    p.OP("ve", lambda E: E.tensor_tensor(R0, R0, R1, ALU.mult))
    p.OP("ve", lambda E: E.tensor_copy(RB2[0:1, 0:W], R1))
    p.OP("ve", lambda E: E.tensor_copy(RB2[0:1, W:2 * W], R0), sem="ve")
    p.W("te", "ve")
    p.OP("te", lambda E: E.matmul(rb, ONES[0:1, :], RB2[0:1, 0:W],
                                  start=True, stop=True))
    p.OP("te", lambda E: E.matmul(mb, ONES[0:1, :], RB2[0:1, W:2 * W],
                                  start=True, stop=True), sem="te")
    p.W("ve", "te")
    for c in range(DC):
        out_cb(c, rb, mb)


def _fm3(hbm_2d, p128=128, w=None):
    return hbm_2d.rearrange("(c p) w -> p c w", p=p128)


def build_launch_A(cfg, stop_after=None, dbg=False):
    D, T, S, DK, H, DH = cfg.D, cfg.T, cfg.S, cfg.DK, cfg.H, cfg.DH
    DC, HC, NT, NTS = cfg.DC, cfg.HC, cfg.NT, cfg.NTS
    HL = H // 2
    WKV = D // 2
    DCq = WKV // 128
    H65 = HL * 65
    JR = D // 64
    MJ = 512 // JR
    RPH = T // HL
    NQ = max(1, D // 512)
    nc = bass.Bass("TRN2", target_bir_lowering=False)

    din = lambda n, s, d=F32: nc.dram_tensor(n, s, d, kind="ExternalInput").ap()
    dout = lambda n, s, d: nc.dram_tensor(n, s, d, kind="ExternalOutput").ap()

    x0T = din("x0T", [D, S])
    x0Tg = din("x0Tg", [D, S])
    wqgT = din("wqgT", [D, WKV])
    wkvT = din("wkvT", [D, D])
    bqF = din("bqF", [128, DCq])
    bkvrow = din("bkvrow", [1, D])
    wprojT = din("wprojT", [D, D])
    bprojF = din("bprojF", [128, DC])
    gammaF = din("gammaF", [128, DC])
    betaF = din("betaF", [128, DC])
    w1T = din("w1T", [D, DH])
    w2T = din("w2T", [D, DH])
    h1T = din("h1T", [DH, DH])
    h2T = din("h2T", [DH, DH])
    w3T = din("w3T", [DH, D])
    s1F = din("s1F", [128, HC])
    s2F = din("s2F", [128, HC])
    sh1F = din("sh1F", [128, HC])
    sh2F = din("sh2F", [128, HC])
    s3F = din("s3F", [128, DC])
    wkT = din("wkT", [D, D])
    wvT = din("wvT", [D, D])
    bkF = din("bkF", [128, DC])
    bvrow = din("bvrow", [1, D])

    x2T_o = dout("x2T", [D, T], F32)
    xn2T_o = dout("xn2T", [D, T], BF16)
    KT_o = dout("KT", [D, T], BF16)
    V_o = dout("V", [T, D], BF16)

    dbgV_o = dout("dbgV", [128, D], BF16) if dbg else None
    dbgV2_o = dout("dbgV2", [128, D], BF16) if dbg else None
    scr_d = nc.dram_tensor("scr_d", [D, T], BF16, kind="Internal").ap()
    xsp = nc.dram_tensor("xsp", [D, T], F32, kind="Internal").ap()
    g_sp = nc.dram_tensor("g_sp", [HC, 128, T], BF16, kind="Internal").ap()
    g2_sp = nc.dram_tensor("g2_sp", [NT, HC, 128, 512], BF16, kind="Internal").ap()

    A2_END = DC * (D + WKV + 3 * D) + DC * 512 + max(DC * 512, 4 * H65 + 64) \
        + DC * 512 + D + DCq * 512 + 1024
    A9_END = HC * D + 2 * DC * D + D + 3 * (DC * 512) + 512
    ARH_COLS = max(HC * T, A2_END, A9_END)
    AR1_COLS = max(DC * T, HC * 512, 2 * T)

    with ExitStack() as ctx:
        sbt = lambda n, s, d: ctx.enter_context(nc.sbuf_tensor(n, s, d))
        AR1 = sbt("AR1", [128, AR1_COLS], BF16)
        ARH = sbt("ARH", [128, ARH_COLS], BF16)
        WBUF = sbt("WBUF", [128, max(2 * DC * 256, HC * 128, DC * 512)], BF16)
        SCRF = sbt("SCRF", [128, DC * 512 + 512 * 2 + 1536], F32)
        KVS2 = sbt("KVS2", [128, H65], BF16)
        KVA = sbt("KVA", [64, H65], F32)
        ONES = sbt("ONES", [128, 128], BF16)
        BIAS = sbt("BIAS", [128, 16 * DC + 4 * HC], F32)
        RB2 = sbt("RB2", [1, 1024], BF16)
        ps = ctx.enter_context(nc.psum_tensor("ps", [128, 4096], F32))
        block = ctx.enter_context(nc.Block())
        p = P(nc, ctx)
        for s in ("syl", "syo", "gpl", "gps", "te", "ve", "sc"):
            p.sem(s)

        bank = lambda i: ps[:, i * 512:(i + 1) * 512]

        XF = 0
        TAc = DC * 512
        TBc = TAc + 512
        RWc = TBc + 512
        R0 = SCRF[0:1, RWc:RWc + 512]
        R1 = SCRF[0:1, RWc + 512:RWc + 1024]
        R2r = SCRF[0:1, RWc + 1024:RWc + 1536]
        TMPA = SCRF[:, TAc:TAc + 512]
        TMPB = SCRF[:, TBc:TBc + 512]

        OBQ, OBP, OGA, OBE, OS3, OBK = 0, DC, 2 * DC, 3 * DC, 4 * DC, 5 * DC
        OS1 = 16 * DC
        OS2, OSH1, OSH2 = OS1 + HC, OS1 + 2 * HC, OS1 + 3 * HC
        bcol = lambda off, i: BIAS[:, off + i:off + i + 1]

        AW_KV = 0
        AW_Q = AW_KV + DC * D
        AW_P = AW_Q + DC * WKV
        AW_K = AW_P + DC * D
        AW_V = AW_K + DC * D
        A_SCR = AW_V + DC * D
        RS_XB = A_SCR
        RS_SQ = RS_XB + DC * 512
        RS_XNS = RS_SQ + max(DC * 512, 4 * H65 + 64)
        RS_BKV = RS_XNS + DC * 512
        RS_Q4 = RS_BKV + D
        ATT0 = RS_Q4 + DCq * 512
        assert ATT0 + 1024 <= ARH_COLS, (ATT0, ARH_COLS)
        W3_0 = 0
        WK_0 = W3_0 + HC * D
        WV_0 = WK_0 + DC * D
        A9_SCR = WV_0 + DC * D
        BV1_0 = A9_SCR
        X2B_0 = BV1_0 + D
        KB_0 = X2B_0 + DC * 512
        VS_0 = KB_0 + DC * 512
        SQ9_0 = VS_0 + 512
        assert SQ9_0 + DC * 512 <= ARH_COLS

        for dst, src in ((BIAS[:, OBQ:OBQ + DCq], bqF),
                         (BIAS[:, OBP:OBP + DC], bprojF),
                         (BIAS[:, OGA:OGA + DC], gammaF),
                         (BIAS[:, OBE:OBE + DC], betaF),
                         (BIAS[:, OS3:OS3 + DC], s3F),
                         (BIAS[:, OBK:OBK + DC], bkF),
                         (BIAS[:, OS1:OS1 + HC], s1F),
                         (BIAS[:, OS2:OS2 + HC], s2F),
                         (BIAS[:, OSH1:OSH1 + HC], sh1F),
                         (BIAS[:, OSH2:OSH2 + HC], sh2F)):
            p.DMA("sy", dst, src, "syl")

        def wload(dstbase, src, ncols):
            dst = ARH[:, dstbase:dstbase + DC * ncols].rearrange(
                "p (c w) -> p c w", w=ncols)
            p.DMA("gp", dst, _fm3(src), "gpl")

        wload(AW_KV, wkvT, D)
        wload(AW_Q, wqgT, WKV)
        wload(AW_P, wprojT, D)
        wload(AW_K, wkT, D)
        wload(AW_V, wvT, D)
        p.DMA("gp", ARH[0:1, RS_BKV:RS_BKV + D], bkvrow, "gpl")
        p.OP("ve", lambda E: E.memset(ONES[:, :], 1.0), sem="ve")
        p.W("te", "gpl")
        p.W("te", "ve")
        nsl = D // 512
        for j in range(nsl):
            p.OP("te", lambda E, j=j: E.matmul(
                bank(j % 4), ONES[0:1, :],
                ARH[0:1, RS_BKV + j * 512:RS_BKV + (j + 1) * 512],
                start=True, stop=True), sem="te" if j == nsl - 1 else None)
        p.W("ve", "te")
        for j in range(nsl):
            p.OP("ve", lambda E, j=j: E.tensor_copy(
                ARH[:, RS_BKV + j * 512:RS_BKV + (j + 1) * 512], bank(j % 4)),
                sem="ve" if j == nsl - 1 else None)
        BKV = ARH[:, RS_BKV:RS_BKV + D]

        for tt in range(NTS):
            own = tt < NT

            def xnc(c, tt=tt, own=own):
                if own:
                    return AR1[:, c * T + tt * 512:c * T + tt * 512 + 512]
                return ARH[:, RS_XNS + c * 512:RS_XNS + (c + 1) * 512]

            p.W("sy", "ve")
            p.DMA("sy", SCRF[:, XF:XF + DC * 512].rearrange("p (c w) -> p c w", w=512),
                  _fm3(x0T[:, tt * 512:(tt + 1) * 512]), "syl")
            p.W("gp", "te")
            p.DMA("gp", ARH[:, RS_XB:RS_XB + DC * 512].rearrange("p (c w) -> p c w", w=512),
                  _fm3(x0T[:, tt * 512:(tt + 1) * 512]), "gpl")
            p.W("ve", "syl")
            p.W("sc", "gpl")
            p.W("te", "gpl")

            def mkcb(xnc=xnc):
                def cb(c, rb, mb):
                    p.OP("ve", lambda E, c=c, rb=rb: E.tensor_tensor(
                        TMPA, SCRF[:, XF + c * 512:XF + (c + 1) * 512], rb, ALU.mult))
                    p.OP("ve", lambda E, c=c, mb=mb, xnc=xnc: E.tensor_tensor(
                        xnc(c), TMPA, mb, ALU.subtract),
                        sem="ve" if c == DC - 1 else None)
                return cb

            emit_ln(p, cfg, 512,
                    lambda c: ARH[:, RS_XB + c * 512:RS_XB + (c + 1) * 512],
                    mkcb(), bank(0)[0:1, :], bank(1)[0:1, :], bank(2), bank(3),
                    ONES, R0, R1, R2r, RB2,
                    lambda c: ARH[:, RS_SQ + c * 512:RS_SQ + (c + 1) * 512])

            vpk = lambda ts: ARH[:, RS_SQ + ts * H65:RS_SQ + (ts + 1) * H65]
            p.W("ve", "te")
            p.OP("ve", lambda E: E.memset(
                ARH[:, RS_SQ:RS_SQ + 4 * H65].rearrange(
                    "p (t h e) -> p (t h) e", h=HL, e=65)[:, :, 64:65], 1.0),
                sem="ve")

            kph = lambda ts: ARH[:, RS_XB + ts * WKV:RS_XB + (ts + 1) * WKV]
            evdone = {}
            for ts in range(4):
                for kvs in range(2):
                    it = ts * 2 + kvs
                    b = it % 2
                    if it >= 2:
                        p.W("te", "ve", evdone[it - 2])
                    if it == 0:
                        p.W("te", "ve")
                    for c in range(DC):
                        p.OP("te", lambda E, ts=ts, kvs=kvs, c=c, b=b, xnc=xnc: E.matmul(
                            ps[:, b * 512:b * 512 + WKV],
                            xnc(c)[:, ts * 128:(ts + 1) * 128],
                            ARH[:, AW_KV + c * D + kvs * WKV:
                                AW_KV + c * D + (kvs + 1) * WKV],
                            start=(c == 0), stop=(c == DC - 1)),
                            sem="te" if c == DC - 1 else None)
                    p.W("ve", "te")
                    p.W("sc", "te")
                    if kvs == 0:
                        p.OP("ve", lambda E, b=b: E.tensor_tensor(
                            TMPA[:, 0:WKV], ps[:, b * 512:b * 512 + WKV],
                            BKV[:, 0:WKV], ALU.add))
                        p.OP("ve", lambda E: E.tensor_scalar_min(
                            TMPB[:, 0:WKV], TMPA[:, 0:WKV], 0.0), sem="ve")
                        p.W("sc", "ve")
                        p.OP("sc", lambda E: E.activation(
                            TMPB[:, 0:WKV], TMPB[:, 0:WKV], AF.Exp), sem="sc")
                        p.W("ve", "sc")
                        p.OP("ve", lambda E, ts=ts, kph=kph: E.scalar_tensor_tensor(
                            kph(ts), TMPA[:, 0:WKV], 0.0, TMPB[:, 0:WKV],
                            ALU.max, ALU.add), sem="ve")
                    else:
                        p.OP("ve", lambda E, ts=ts, b=b, vpk=vpk: E.tensor_tensor(
                            vpk(ts).rearrange("p (h e) -> p h e", e=65)
                            [:, :, 0:64],
                            ps[:, b * 512:b * 512 + WKV].rearrange(
                                "p (h e) -> p h e", e=64),
                            BKV[:, WKV:2 * WKV].rearrange("p (h e) -> p h e", e=64),
                            ALU.add),
                            sem="ve")
                    evdone[it] = p.cnt["ve"]
            p.W("te", "ve")
            for h in range(HL):
                bb, off = 4 + h // 4, (h % 4) * 65
                for ts in range(4):
                    p.OP("te", lambda E, ts=ts, h=h, bb=bb, off=off, kph=kph, vpk=vpk: E.matmul(
                        ps[0:64, bb * 512 + off:bb * 512 + off + 65],
                        kph(ts)[:, h * 64:(h + 1) * 64],
                        vpk(ts)[:, h * 65:(h + 1) * 65],
                        start=(ts == 0), stop=(ts == 3)),
                        sem="te" if (ts == 3 and h == HL - 1) else None)
            p.W("ve", "te")
            for bb in range(4, 4 + (HL + 3) // 4):
                w4 = min(4, HL - (bb - 4) * 4) * 65
                co = (bb - 4) * 4 * 65
                if tt == 0:
                    p.OP("ve", lambda E, bb=bb, w4=w4, co=co: E.tensor_copy(
                        KVA[0:64, co:co + w4],
                        ps[0:64, bb * 512:bb * 512 + w4]))
                else:
                    p.OP("ve", lambda E, bb=bb, w4=w4, co=co: E.tensor_tensor(
                        KVA[0:64, co:co + w4], KVA[0:64, co:co + w4],
                        ps[0:64, bb * 512:bb * 512 + w4], ALU.add))

        p.OP("ve", lambda E: E.tensor_copy(KVS2[0:64, :], KVA[0:64, :]))
        p.W("gp", "ve")
        p.DMA("gp", KVS2[64:128, :], KVS2[0:64, :], "gps")
        p.W("te", "gps")

        if stop_after == "A2":
            p.W("sy", "gps")
            p.emit(block)
            return nc
        qph = lambda c: ARH[:, RS_Q4 + c * 512:RS_Q4 + (c + 1) * 512]
        xnt = lambda c: ARH[:, RS_XNS + c * 512:RS_XNS + (c + 1) * 512]
        att0 = lambda s: ARH[:, ATT0 + s * 512:ATT0 + (s + 1) * 512]

        for at in range(NTS):
            p.W("sy", "ve")
            p.W("sy", "syo")
            p.DMA("sy", SCRF[:, XF:XF + DC * 512].rearrange("p (c w) -> p c w", w=512),
                  _fm3(x0Tg[:, at * 512:(at + 1) * 512]), "syl")
            p.W("gp", "te")
            p.W("gp", "ve")
            p.DMA("gp", ARH[:, RS_SQ:RS_SQ + DC * 512].rearrange("p (c w) -> p c w", w=512),
                  _fm3(x0Tg[:, at * 512:(at + 1) * 512]), "gpl")
            p.W("ve", "syl")
            p.W("sc", "gpl")
            p.W("te", "gpl")

            def cb_xn(c, rb, mb):
                p.OP("ve", lambda E, c=c, rb=rb: E.tensor_tensor(
                    TMPA, SCRF[:, XF + c * 512:XF + (c + 1) * 512], rb, ALU.mult))
                p.OP("ve", lambda E, c=c, mb=mb: E.tensor_tensor(
                    xnt(c), TMPA, mb, ALU.subtract),
                    sem="ve" if c == DC - 1 else None)

            emit_ln(p, cfg, 512,
                    lambda c: ARH[:, RS_SQ + c * 512:RS_SQ + (c + 1) * 512],
                    cb_xn, bank(0)[0:1, :], bank(1)[0:1, :], bank(2), bank(3),
                    ONES, R0, R1, R2r, RB2,
                    lambda c: ARH[:, RS_XB + c * 512:RS_XB + (c + 1) * 512])

            evq = {}
            for qc in range(DCq):
                b = qc % 2
                if qc >= 2:
                    p.W("te", "ve", evq[qc - 2])
                else:
                    p.W("te", "ve")
                for c in range(DC):
                    p.OP("te", lambda E, qc=qc, c=c, b=b: E.matmul(
                        bank(b),
                        ARH[:, AW_Q + c * WKV + qc * 128:AW_Q + c * WKV + (qc + 1) * 128],
                        xnt(c), start=(c == 0), stop=(c == DC - 1)),
                        sem="te" if c == DC - 1 else None)
                p.W("ve", "te")
                p.OP("ve", lambda E, b=b, qc=qc: E.tensor_scalar(
                    TMPB, bank(b), bcol(OBQ, qc), 0.0, ALU.add, ALU.min), sem="ve")
                p.W("sc", "ve")
                p.OP("sc", lambda E: E.activation(TMPB, TMPB, AF.Exp), sem="sc")
                p.OP("ve", lambda E, b=b, qc=qc: E.tensor_scalar(
                    TMPA, bank(b), bcol(OBQ, qc), 0.0, ALU.add, ALU.max))
                p.W("ve", "sc")
                p.OP("ve", lambda E, qc=qc: E.tensor_tensor(
                    qph(qc), TMPA, TMPB, ALU.add), sem="ve")
                evq[qc] = p.cnt["ve"]

            for hl in range(HL):
                B = (hl % 2) * 64
                b4 = 4 + hl % 2
                slot = hl % 2
                p.W("te", "ve")
                p.W("te", "sc")
                p.OP("te", lambda E, hl=hl, B=B, b4=b4: E.matmul(
                    ps[0:65, b4 * 512:b4 * 512 + 512],
                    KVS2[B:B + 64, hl * 65:(hl + 1) * 65],
                    qph(hl // 2)[B:B + 64, :],
                    start=True, stop=True), sem="te")
                p.W("ve", "te")
                p.OP("ve", lambda E, b4=b4: E.tensor_scalar_max(
                    SCRF[64:65, TAc:TAc + 512],
                    ps[64:65, b4 * 512:b4 * 512 + 512], 1e-6))
                p.OP("ve", lambda E: E.reciprocal(
                    SCRF[64:65, TAc:TAc + 512], SCRF[64:65, TAc:TAc + 512]))
                p.OP("ve", lambda E: E.tensor_copy(
                    TMPB.bitcast(BF16)[64:65, 0:512], SCRF[64:65, TAc:TAc + 512]),
                    sem="ve")
                p.W("te", "ve")
                p.OP("te", lambda E: E.matmul(
                    ps[0:128, 6 * 512:7 * 512], ONES[64:65, :],
                    TMPB.bitcast(BF16)[64:65, 0:512], start=True, stop=True),
                    sem="te")
                p.W("sc", "te")
                p.OP("sc", lambda E, b4=b4: E.activation(
                    TMPA.bitcast(BF16)[0:64, 0:512],
                    ps[0:64, b4 * 512:b4 * 512 + 512], AF.Copy), sem="sc")
                p.W("ve", "sc")
                p.W("ve", "gps")
                p.OP("ve", lambda E, slot=slot: E.tensor_tensor(
                    att0(slot)[0:64, :].rearrange("p (a b) -> p b a", a=JR),
                    TMPA.bitcast(BF16)[0:64, 0:512],
                    ps[0:64, 6 * 512:6 * 512 + 512], ALU.mult), sem="ve")
                p.W("gp", "ve")
                dst = bass.AP(tensor=scr_d.tensor,
                              offset=hl * RPH + at * MJ,
                              ap=[[T, 64], [64 * T, JR], [1, MJ]])
                p.DMA("gp", dst,
                      att0(slot)[0:64, :].rearrange("p (a b) -> p a b", a=JR),
                      "gps")

        if stop_after == "A4a":
            p.W("sy", "gps")
            p.emit(block)
            return nc

        att = lambda c: ARH[:, RS_SQ + c * 512:RS_SQ + (c + 1) * 512]
        prj = lambda c: ARH[:, RS_XNS + c * 512:RS_XNS + (c + 1) * 512]
        mo = qph
        SQ4 = att

        for tt in range(NT):
            p.W("sy", "ve")
            p.W("sy", "syo")
            p.DMA("sy", SCRF[:, XF:XF + DC * 512].rearrange("p (c w) -> p c w", w=512),
                  _fm3(x0T[:, tt * 512:(tt + 1) * 512]), "syl")
            p.W("gp", "te")
            p.W("gp", "ve")
            p.W("gp", "gps")
            p.DMA("gp", ARH[:, RS_SQ:RS_SQ + DC * 512].rearrange("p (c w) -> p c w", w=512),
                  _fm3(scr_d[:, tt * 512:(tt + 1) * 512]), "gps")
            p.W("ve", "syl")
            p.W("te", "gps")

            evo = {}
            for oc in range(DC):
                b = oc % 2
                if oc >= 2:
                    p.W("te", "sc", evo[oc - 2])
                else:
                    p.W("te", "ve")
                for c in range(DC):
                    p.OP("te", lambda E, oc=oc, c=c, b=b: E.matmul(
                        bank(b),
                        ARH[:, AW_P + c * D + oc * 128:AW_P + c * D + (oc + 1) * 128],
                        att(c), start=(c == 0), stop=(c == DC - 1)),
                        sem="te" if c == DC - 1 else None)
                p.W("sc", "te")
                p.OP("sc", lambda E, oc=oc, b=b: E.activation(
                    prj(oc), bank(b), AF.Identity, bias=bcol(OBP, oc)), sem="sc")
                evo[oc] = p.cnt["sc"]

            def cb_inner(c, rb, mb, tt=tt):
                p.OP("ve", lambda E, c=c, rb=rb: E.tensor_tensor(
                    TMPA, prj(c), rb, ALU.mult))
                p.OP("ve", lambda E, c=c, mb=mb: E.tensor_tensor(
                    TMPA, TMPA, mb, ALU.subtract))
                p.OP("ve", lambda E, c=c: E.tensor_scalar(
                    TMPA, TMPA, bcol(OGA, c), bcol(OBE, c), ALU.mult, ALU.add))
                p.OP("ve", lambda E, c=c: E.tensor_tensor(
                    mo(c), TMPA, AR1[:, c * T + tt * 512:c * T + tt * 512 + 512],
                    ALU.add), sem="ve" if c == DC - 1 else None)

            emit_ln(p, cfg, 512, lambda c: prj(c), cb_inner,
                    bank(0)[0:1, :], bank(1)[0:1, :], bank(2), bank(3),
                    ONES, R0, R1, R2r, RB2, lambda c: SQ4(c))

            def cb_outer(c, rb, mb):
                p.OP("ve", lambda E, c=c, rb=rb: E.tensor_tensor(
                    TMPA, mo(c), rb, ALU.mult))
                p.OP("ve", lambda E, c=c, mb=mb: E.tensor_tensor(
                    TMPA, TMPA, mb, ALU.subtract))
                p.OP("ve", lambda E, c=c: E.tensor_tensor(
                    SCRF[:, XF + c * 512:XF + (c + 1) * 512],
                    TMPA, SCRF[:, XF + c * 512:XF + (c + 1) * 512], ALU.add))
                p.OP("ve", lambda E, c=c: E.tensor_copy(
                    prj(c), SCRF[:, XF + c * 512:XF + (c + 1) * 512]),
                    sem="ve" if c == DC - 1 else None)

            emit_ln(p, cfg, 512, lambda c: mo(c), cb_outer,
                    bank(0)[0:1, :], bank(1)[0:1, :], bank(2), bank(3),
                    ONES, R0, R1, R2r, RB2, lambda c: SQ4(c))

            p.W("sy", "ve")
            p.DMA("sy", xsp[:, tt * 512:(tt + 1) * 512].rearrange("(c p) w -> p c w", p=128),
                  SCRF[:, XF:XF + DC * 512].rearrange("p (c w) -> p c w", w=512), "syo")

            def cb_ffn(c, rb, mb, tt=tt):
                p.OP("ve", lambda E, c=c, rb=rb: E.tensor_tensor(
                    TMPA, prj(c), rb, ALU.mult))
                p.OP("ve", lambda E, c=c, mb=mb: E.tensor_tensor(
                    AR1[:, c * T + tt * 512:c * T + tt * 512 + 512],
                    TMPA, mb, ALU.subtract), sem="ve" if c == DC - 1 else None)

            emit_ln(p, cfg, 512, lambda c: prj(c), cb_ffn,
                    bank(0)[0:1, :], bank(1)[0:1, :], bank(2), bank(3),
                    ONES, R0, R1, R2r, RB2, lambda c: SQ4(c))

        if stop_after == "A4":
            p.W("sy", "syo")
            p.emit(block)
            return nc
        p.W("gp", "ve")
        p.W("gp", "te")
        hslot = lambda o, ts: ARH[:, o * T + ts * 512:o * T + (ts + 1) * 512]
        ev1 = {}
        git = 0
        for og in range(DH // 256):
            p.W("gp", "te")
            p.DMA("gp", WBUF[:, 0:DC * 256].rearrange("p (c w) -> p c w", w=256),
                  _fm3(w1T[:, og * 256:(og + 1) * 256]), "gpl")
            p.DMA("gp", WBUF[:, DC * 256:2 * DC * 256].rearrange("p (c w) -> p c w", w=256),
                  _fm3(w2T[:, og * 256:(og + 1) * 256]), "gpl")
            p.W("te", "gpl")
            for o4 in range(2):
                o = og * 2 + o4
                for ts in range(NT):
                    it = git
                    git += 1
                    pa = (it % 2) * 2
                    if it >= 2:
                        p.W("te", "ve", ev1[it - 2])
                    elif it == 0:
                        p.W("te", "ve")
                        p.W("te", "sc")
                    for c in range(DC):
                        p.OP("te", lambda E, c=c, o4=o4, ts=ts, pa=pa: E.matmul(
                            bank(pa),
                            WBUF[:, c * 256 + o4 * 128:c * 256 + (o4 + 1) * 128],
                            AR1[:, c * T + ts * 512:c * T + (ts + 1) * 512],
                            start=(c == 0), stop=(c == DC - 1)))
                    for c in range(DC):
                        p.OP("te", lambda E, c=c, o4=o4, ts=ts, pa=pa: E.matmul(
                            bank(pa + 1),
                            WBUF[:, DC * 256 + c * 256 + o4 * 128:
                                 DC * 256 + c * 256 + (o4 + 1) * 128],
                            AR1[:, c * T + ts * 512:c * T + (ts + 1) * 512],
                            start=(c == 0), stop=(c == DC - 1)),
                            sem="te" if c == DC - 1 else None)
                    p.W("sc", "te")
                    p.OP("sc", lambda E, o=o, ts=ts, pa=pa: E.activation(
                        hslot(o, ts), bank(pa), AF.Silu, scale=bcol(OS1, o)),
                        sem="sc")
                    p.W("ve", "sc")
                    p.OP("ve", lambda E, o=o, ts=ts, pa=pa: E.scalar_tensor_tensor(
                        hslot(o, ts), bank(pa + 1), bcol(OS2, o), hslot(o, ts),
                        ALU.mult, ALU.mult), sem="ve")
                    ev1[it] = p.cnt["ve"]

        def ffn_mid(wsrc, scol_off, out_cb_o):
            p.W("te", "ve")
            p.W("te", "sc")
            for o2 in range(DH // 128):
                p.W("gp", "te")
                p.DMA("gp", WBUF[:, 0:HC * 128].rearrange("p (c w) -> p c w", w=128),
                      wsrc[:, o2 * 128:(o2 + 1) * 128].rearrange("(c p) w -> p c w", p=128),
                      "gpl")
                p.W("te", "gpl")
                for ts in range(NT):
                    b = ts % 2
                    p.W("te", "sc")
                    for c in range(HC):
                        p.OP("te", lambda E, c=c, ts=ts, b=b: E.matmul(
                            bank(b),
                            WBUF[:, c * 128:(c + 1) * 128],
                            ARH[:, c * T + ts * 512:c * T + (ts + 1) * 512],
                            start=(c == 0), stop=(c == HC - 1)),
                            sem="te" if c == HC - 1 else None)
                    p.W("sc", "te")
                    out_cb_o(o2, ts, b, scol_off)

        gstage = lambda o2: AR1[:, (o2 % 2) * T:(o2 % 2) * T + T]

        def g_out(o2, ts, b, scol_off):
            if ts == 0:
                p.W("sc", "gps")
            p.OP("sc", lambda E, o2=o2, ts=ts, b=b: E.activation(
                gstage(o2)[:, ts * 512:(ts + 1) * 512], bank(b), AF.Gelu,
                scale=bcol(scol_off, o2)), sem="sc")
            if ts == NT - 1:
                p.W("gp", "sc")
                p.DMA("gp", g_sp[o2], gstage(o2), "gps")

        ffn_mid(h1T, OSH1, g_out)

        p.W("sy", "gps")
        p.W("sy", "te")
        p.DMA("sy", ARH[:, 0:HC * T].rearrange("p (c w) -> p c w", w=T),
              g_sp.rearrange("c p w -> p c w"), "syl")
        p.W("te", "syl")

        def g2_out(o3, ts, b, scol_off):
            slot = (o3 * NT + ts) % 2
            p.W("sc", "gps")
            p.OP("sc", lambda E, o3=o3, ts=ts, b=b, slot=slot: E.activation(
                AR1[:, slot * 512:(slot + 1) * 512], bank(b), AF.Gelu,
                scale=bcol(scol_off, o3)), sem="sc")
            p.W("gp", "sc")
            p.DMA("gp", g2_sp[ts, o3], AR1[:, slot * 512:(slot + 1) * 512], "gps")

        ffn_mid(h2T, OSH2, g2_out)

        p.W("gp", "te")
        p.W("gp", "sc")
        p.DMA("gp", ARH[:, W3_0:W3_0 + HC * D].rearrange("p (c w) -> p c w", w=D),
              w3T.rearrange("(c p) w -> p c w", p=128), "gpl")
        p.DMA("gp", ARH[:, WK_0:WK_0 + DC * D].rearrange("p (c w) -> p c w", w=D),
              _fm3(wkT), "gpl")
        p.DMA("gp", ARH[:, WV_0:WV_0 + DC * D].rearrange("p (c w) -> p c w", w=D),
              _fm3(wvT), "gpl")
        p.DMA("gp", ARH[0:1, BV1_0:BV1_0 + D], bvrow, "gpl")
        p.W("te", "gpl")
        nslv = D // 512
        for j in range(nslv):
            p.OP("te", lambda E, j=j: E.matmul(
                bank(j % 2), ONES[0:1, :],
                ARH[0:1, BV1_0 + j * 512:BV1_0 + (j + 1) * 512],
                start=True, stop=True), sem="te" if j == nslv - 1 else None)
        p.W("ve", "te")
        for j in range(nslv):
            p.OP("ve", lambda E, j=j: E.tensor_copy(
                ARH[:, BV1_0 + j * 512:BV1_0 + (j + 1) * 512], bank(j % 2)),
                sem="ve" if j == nslv - 1 else None)
        BV1 = ARH[:, BV1_0:BV1_0 + D]
        if dbg:
            p.W("sy", "ve")
            p.DMA("sy", dbgV_o, BV1, "syo")

        ybf = lambda oc: WBUF[:, oc * 512:(oc + 1) * 512]
        xn2 = lambda c: WBUF[:, c * 512:(c + 1) * 512]
        x2b = lambda c: ARH[:, X2B_0 + c * 512:X2B_0 + (c + 1) * 512]
        kbf = lambda kc: ARH[:, KB_0 + kc * 512:KB_0 + (kc + 1) * 512]
        sq9 = lambda c: ARH[:, SQ9_0 + c * 512:SQ9_0 + (c + 1) * 512]

        for ts in range(NT):
            p.W("sy", "gps")
            p.W("sy", "syo")
            p.W("sy", "te")
            p.W("sy", "ve")
            p.DMA("sy", AR1[:, 0:HC * 512].rearrange("p (c w) -> p c w", w=512),
                  g2_sp[ts].rearrange("c p w -> p c w"), "syl")
            p.DMA("sy", SCRF[:, XF:XF + DC * 512].rearrange("p (c w) -> p c w", w=512),
                  xsp[:, ts * 512:(ts + 1) * 512].rearrange("(c p) w -> p c w", p=128),
                  "syl")
            p.W("te", "syl")
            p.W("te", "ve")
            p.W("te", "sc")
            for oc in range(DC):
                for c in range(HC):
                    p.OP("te", lambda E, oc=oc, c=c: E.matmul(
                        bank(oc),
                        ARH[:, W3_0 + c * D + oc * 128:W3_0 + c * D + (oc + 1) * 128],
                        AR1[:, c * 512:(c + 1) * 512],
                        start=(c == 0), stop=(c == HC - 1)),
                        sem="te" if (c == HC - 1 and oc == DC - 1) else None)
            p.W("sc", "te")
            for oc in range(DC):
                p.OP("sc", lambda E, oc=oc: E.activation(
                    ybf(oc), bank(oc), AF.Copy, scale=bcol(OS3, oc)),
                    sem="sc" if oc == DC - 1 else None)

            def cb_x2(c, rb, mb):
                p.OP("ve", lambda E, c=c, rb=rb: E.tensor_tensor(
                    TMPA, ybf(c), rb, ALU.mult))
                p.OP("ve", lambda E, c=c, mb=mb: E.tensor_tensor(
                    TMPA, TMPA, mb, ALU.subtract))
                p.OP("ve", lambda E, c=c: E.tensor_tensor(
                    SCRF[:, XF + c * 512:XF + (c + 1) * 512],
                    TMPA, SCRF[:, XF + c * 512:XF + (c + 1) * 512], ALU.add))
                p.OP("ve", lambda E, c=c: E.tensor_copy(
                    x2b(c), SCRF[:, XF + c * 512:XF + (c + 1) * 512]),
                    sem="ve" if c == DC - 1 else None)

            emit_ln(p, cfg, 512, lambda c: ybf(c), cb_x2,
                    bank(0)[0:1, :], bank(1)[0:1, :], bank(2), bank(3),
                    ONES, R0, R1, R2r, RB2, sq9)

            p.W("sy", "ve")
            p.DMA("sy", x2T_o[:, ts * 512:(ts + 1) * 512].rearrange("(c p) w -> p c w", p=128),
                  SCRF[:, XF:XF + DC * 512].rearrange("p (c w) -> p c w", w=512), "syo")

            def cb_xn2(c, rb, mb):
                p.OP("ve", lambda E, c=c, rb=rb: E.tensor_tensor(
                    TMPA, x2b(c), rb, ALU.mult))
                p.OP("ve", lambda E, c=c, mb=mb: E.tensor_tensor(
                    xn2(c), TMPA, mb, ALU.subtract),
                    sem="ve" if c == DC - 1 else None)

            emit_ln(p, cfg, 512, lambda c: x2b(c), cb_xn2,
                    bank(0)[0:1, :], bank(1)[0:1, :], bank(2), bank(3),
                    ONES, R0, R1, R2r, RB2, sq9)
            p.W("sy", "ve")
            p.DMA("sy", xn2T_o[:, ts * 512:(ts + 1) * 512].rearrange("(c p) w -> p c w", p=128),
                  WBUF[:, 0:DC * 512].rearrange("p (c w) -> p c w", w=512),
                  "syo")

            evk = {}
            for kc in range(DC):
                b = kc % 2
                if kc >= 2:
                    p.W("te", "sc", evk[kc - 2])
                else:
                    p.W("te", "ve")
                for c in range(DC):
                    p.OP("te", lambda E, kc=kc, c=c, b=b: E.matmul(
                        bank(b),
                        ARH[:, WK_0 + c * D + kc * 128:WK_0 + c * D + (kc + 1) * 128],
                        xn2(c), start=(c == 0), stop=(c == DC - 1)),
                        sem="te" if c == DC - 1 else None)
                p.W("sc", "te")
                p.OP("sc", lambda E, kc=kc, b=b: E.activation(
                    kbf(kc), bank(b), AF.Identity, bias=bcol(OBK, kc)), sem="sc")
                evk[kc] = p.cnt["sc"]
            p.W("sy", "sc")
            p.DMA("sy", KT_o[:, ts * 512:(ts + 1) * 512].rearrange("(c p) w -> p c w", p=128),
                  ARH[:, KB_0:KB_0 + DC * 512].rearrange("p (c w) -> p c w", w=512),
                  "syo")

            evv = {}
            for sub in range(4):
                for vs in range(NQ):
                    it = sub * NQ + vs
                    b = 2 + it % 2
                    if it >= 2:
                        p.W("te", "ve", evv[it - 2])
                    for c in range(DC):
                        p.OP("te", lambda E, sub=sub, vs=vs, c=c, b=b: E.matmul(
                            bank(b),
                            xn2(c)[:, sub * 128:(sub + 1) * 128],
                            ARH[:, WV_0 + c * D + vs * 512:WV_0 + c * D + (vs + 1) * 512],
                            start=(c == 0), stop=(c == DC - 1)),
                            sem="te" if c == DC - 1 else None)
                    p.W("ve", "te")
                    p.OP("ve", lambda E, b=b, vs=vs: E.tensor_tensor(
                        ARH[:, VS_0:VS_0 + 512], bank(b),
                        BV1[:, vs * 512:(vs + 1) * 512],
                        ALU.add), sem="ve")
                    evv[it] = p.cnt["ve"]
                    p.W("sy", "ve")
                    p.DMA("sy", V_o[ts * 512 + sub * 128:ts * 512 + (sub + 1) * 128,
                                    vs * 512:(vs + 1) * 512],
                          ARH[:, VS_0:VS_0 + 512], "syo")
                    p.W("ve", "syo")

        if dbg:
            p.W("sy", "ve")
            p.W("sy", "sc")
            p.DMA("sy", dbgV2_o, BV1, "syo")
        p.W("sy", "syo")
        p.W("sy", "gps")
        p.emit(block)
    return nc



def build_launch_B(cfg, stop_after=None, dbg=False):
    D, T, S, DK, H, DH = cfg.D, cfg.T, cfg.S, cfg.DK, cfg.H, cfg.DH
    DC, HC, NT, NTS = cfg.DC, cfg.HC, cfg.NT, cfg.NTS
    NQS = T // 128
    NKC = S // 128
    NKG = S // 1024
    P2 = T + 384
    UCX = 256 + T
    nc = bass.Bass("TRN2", target_bir_lowering=False)

    din = lambda n, s, d=F32: nc.dram_tensor(n, s, d, kind="ExternalInput").ap()
    dout = lambda n, s, d: nc.dram_tensor(n, s, d, kind="ExternalOutput").ap()

    x2T = din("x2T", [D, T])
    xn2T = din("xn2T", [D, T], BF16)
    KTp = din("KTp", [D, S], BF16)
    Vp = din("Vp", [S, D], BF16)
    erxT = din("erxT", [64, T + 1])
    wqT = din("wqT", [D, D])
    bqF = din("bqF", [128, DC])
    woT = din("woT", [D, D])
    boF = din("boF", [128, DC])
    idn = din("idn", [128, 128], BF16)
    fw1T = din("fw1T", [D, DH])
    fw2T = din("fw2T", [D, DH])
    fh1T = din("fh1T", [DH, DH])
    fh2T = din("fh2T", [DH, DH])
    fw3T = din("fw3T", [DH, D])
    fs1F = din("fs1F", [128, HC])
    fs2F = din("fs2F", [128, HC])
    fsh1F = din("fsh1F", [128, HC])
    fsh2F = din("fsh2F", [128, HC])
    fs3F = din("fs3F", [128, DC])

    x4T_o = dout("x4T", [D, T], F32)
    dbgX3_o = dout("dbgX3", [D, T], F32) if dbg else None
    dbgAtt_o = dout("dbgAtt", [D, 512], BF16) if dbg else None

    scr_d = nc.dram_tensor("scr_d", [D, T], BF16, kind="Internal").ap()
    xsp = nc.dram_tensor("xsp", [D, T], F32, kind="Internal").ap()
    g_sp = nc.dram_tensor("g_sp", [HC, 128, T], BF16, kind="Internal").ap()
    g2_sp = nc.dram_tensor("g2_sp", [NT, HC, 128, 512], BF16, kind="Internal").ap()
    r2d = nc.dram_tensor("r2d", [H, T, P2], BF16, kind="Internal")

    BW_Q = 0
    BV_P = BW_Q + DC * D
    BR_D = BV_P + NKC * 130
    RWID0 = sum(min(T - s * 128, T) for s in range(4))
    BE_X = BR_D + 2 * (RWID0 + 512)
    BK_S = BE_X + 4096
    BA_T = BK_S + S
    BSC = BA_T + DC * 512
    B_END = BSC + 3 * DC * 512
    B9_END = HC * D + 3 * DC * 512 + 512
    ARH_COLS = max(HC * T, B_END, B9_END, DC * (256 + 256) + 64)
    AR1_COLS = max(DC * T, HC * 512, 2 * T)

    with ExitStack() as ctx:
        sbt = lambda n, s, d: ctx.enter_context(nc.sbuf_tensor(n, s, d))
        AR1 = sbt("AR1", [128, AR1_COLS], BF16)
        ARH = sbt("ARH", [128, ARH_COLS], BF16)
        WBUF = sbt("WBUF", [128, max(2 * DC * 256, HC * 128, DC * 512)], BF16)
        SCRF = sbt("SCRF", [128, DC * 512 + 512 * 2 + 1536], F32)
        UCR2 = sbt("UCR2", [128, UCX], BF16)
        IDN = sbt("IDN", [128, 128], BF16)
        ONES = sbt("ONES", [128, 128], BF16)
        BIAS = sbt("BIAS", [128, 16 * DC + 4 * HC], F32)
        RB2 = sbt("RB2", [1, 1024], BF16)
        ERX = ARH.ap().bitcast(F32)[0:64, (BA_T // 2):(BA_T // 2) + T + 1] \
            if False else None
        ps = ctx.enter_context(nc.psum_tensor("ps", [128, 4096], F32))
        block = ctx.enter_context(nc.Block())
        p = P(nc, ctx)
        ERX = ARH[:, BA_T:BA_T + 2 * (T + 1) + 2].bitcast(F32)[0:64, 0:T + 1]
        for s in ("syl", "syo", "gpl", "gps", "te", "ve", "sc"):
            p.sem(s)

        bank = lambda i: ps[:, i * 512:(i + 1) * 512]
        XF = 0
        TAc = DC * 512
        TBc = TAc + 512
        RWc = TBc + 512
        R0 = SCRF[0:1, RWc:RWc + 512]
        R1 = SCRF[0:1, RWc + 512:RWc + 1024]
        R2r = SCRF[0:1, RWc + 1024:RWc + 1536]
        TMPA = SCRF[:, TAc:TAc + 512]
        TMPB = SCRF[:, TBc:TBc + 512]

        OBQ, OBO, OS3 = 0, DC, 4 * DC
        OS1 = 16 * DC
        OS2, OSH1, OSH2 = OS1 + HC, OS1 + 2 * HC, OS1 + 3 * HC
        bcol = lambda off, i: BIAS[:, off + i:off + i + 1]

        for dst, srcp in ((BIAS[:, OBQ:OBQ + DC], bqF),
                          (BIAS[:, OBO:OBO + DC], boF),
                          (BIAS[:, OS3:OS3 + DC], fs3F),
                          (BIAS[:, OS1:OS1 + HC], fs1F),
                          (BIAS[:, OS2:OS2 + HC], fs2F),
                          (BIAS[:, OSH1:OSH1 + HC], fsh1F),
                          (BIAS[:, OSH2:OSH2 + HC], fsh2F)):
            p.DMA("sy", dst, srcp, "syl")
        p.DMA("sy", ERX[:, :], erxT, "syl")
        p.DMA("sy", IDN[:, :], idn, "syl")
        p.DMA("gp", ARH[:, BW_Q:BW_Q + DC * D].rearrange("p (c w) -> p c w", w=D),
              _fm3(wqT), "gpl")
        p.OP("ve", lambda E: E.memset(ONES[:, :], 1.0))
        p.OP("ve", lambda E: E.memset(UCR2[:, 0:256], 0.0))
        p.W("ve", "syl")
        p.OP("ve", lambda E: E.tensor_scalar(
            UCR2[0:64, 256:256 + T], ERX[0:64, 0:T], ERX[0:64, T:T + 1], None,
            ALU.subtract))
        p.W("gp", "ve")
        p.DMA("gp", UCR2[64:128, :], UCR2[0:64, :], "gps")

        p.W("te", "gpl")
        p.W("te", "gps")
        xntmp = lambda c: ARH[:, BR_D + c * 512:BR_D + (c + 1) * 512]
        for tt in range(NT):
            p.W("sy", "ve")
            p.W("sy", "sc")
            p.DMA("sy", ARH[:, BR_D:BR_D + DC * 512].rearrange("p (c w) -> p c w", w=512),
                  _fm3(xn2T[:, tt * 512:(tt + 1) * 512]), "syl")
            p.W("te", "syl")
            evq = {}
            for qc in range(DC):
                b = qc % 2
                if qc >= 2:
                    p.W("te", "sc", evq[qc - 2])
                for c in range(DC):
                    p.OP("te", lambda E, qc=qc, c=c, b=b: E.matmul(
                        bank(b),
                        ARH[:, BW_Q + c * D + qc * 128:BW_Q + c * D + (qc + 1) * 128],
                        xntmp(c), start=(c == 0), stop=(c == DC - 1)),
                        sem="te" if c == DC - 1 else None)
                p.W("sc", "te")
                p.OP("sc", lambda E, qc=qc, tt=tt, b=b: E.activation(
                    AR1[:, qc * T + tt * 512:qc * T + (tt + 1) * 512],
                    bank(b), AF.Identity, bias=bcol(OBQ, qc)))
                evq[qc] = p.cnt["sc"]

        for h in range(H):
            B = (h % 2) * 64
            c_h = h // 2
            for t128 in range(NQS):
                lq0 = t128 * 128
                Wt = T + 256 - lq0
                offs, widths, o = [], [], 0
                while o < Wt:
                    w = min(512, Wt - o)
                    offs.append(o); widths.append(w); o += w
                for i, (o0, w) in enumerate(zip(offs, widths)):
                    b = i % 2
                    p.W("te", "sc")
                    p.OP("te", lambda E, h=h, B=B, c_h=c_h, lq0=lq0, o0=o0, w=w, b=b:
                         E.matmul(ps[0:128, b * 512:b * 512 + w],
                                  AR1[B:B + 64, c_h * T + lq0:c_h * T + lq0 + 128],
                                  UCR2[B:B + 64, o0:o0 + w],
                                  start=True, stop=True), sem="te")
                    p.W("sc", "te")
                    p.W("sc", "gps")
                    p.OP("sc", lambda E, b=b, w=w: E.activation(
                        TMPA.bitcast(BF16)[:, 0:w] if b == 0 else
                        TMPB.bitcast(BF16)[:, 0:w],
                        ps[0:128, b * 512:b * 512 + w], AF.Copy))
                    p.W("gp", "sc")
                    diag = bass.AP(tensor=r2d.ap().tensor,
                                   offset=h * T * P2 + lq0 * (P2 + 1) + o0,
                                   ap=[[P2 + 1, 128], [1, w]])
                    p.DMA("gp", diag,
                          TMPA.bitcast(BF16)[:, 0:w] if b == 0 else
                          TMPB.bitcast(BF16)[:, 0:w], "gps")
        if stop_after == "B3":
            p.W("sy", "gps")
            p.emit(block)
            return nc

        p.W("gp", "te")
        p.DMA("gp", ARH[:, BW_Q:BW_Q + DC * D].rearrange("p (c w) -> p c w", w=D),
              _fm3(woT), "gpl")

        vpk = lambda kc, hh: ARH[:, BV_P + kc * 130 + hh * 65:BV_P + kc * 130 + (hh + 1) * 65]
        att = lambda c: ARH[:, BA_T + c * 512:BA_T + (c + 1) * 512]
        prj = lambda c: ARH[:, BSC + c * 512:BSC + (c + 1) * 512]
        sqb = lambda c: ARH[:, BSC + DC * 512 + c * 512:BSC + DC * 512 + (c + 1) * 512]
        x3b = lambda c: ARH[:, BSC + 2 * DC * 512 + c * 512:BSC + 2 * DC * 512 + (c + 1) * 512]

        for qt in range(NT):
            for pr in range(H // 2):
                h, h2 = 2 * pr, 2 * pr + 1
                p.W("sy", "te")
                p.W("sy", "sc")
                p.DMA("sy", ARH[:, BK_S:BK_S + S],
                      KTp[pr * 128:(pr + 1) * 128, :], "syl")
                p.W("gp", "te")
                p.W("gp", "ve")
                for hh in range(2):
                    p.DMA("gp", ARH[:, BV_P:BV_P + NKC * 130].rearrange(
                        "p (kc e) -> p kc e", e=130)[:, :, hh * 65:hh * 65 + 64],
                        Vp[:, (2 * pr + hh) * 64:(2 * pr + hh) * 64 + 64].rearrange(
                            "(kc p) e -> p kc e", p=128), "gpl")
                p.OP("ve", lambda E: E.memset(ARH[:, BV_P:BV_P + NKC * 130].rearrange(
                    "p (kc e) -> p kc e", e=65)[:, :, 64:65], 1.0))
                rect_off = {}
                ro = 0
                p.W("gp", "gps")
                for hh in range(2):
                    for s4 in range(4):
                        lq0 = qt * 512 + s4 * 128
                        wid = T - lq0
                        if wid <= 0:
                            continue
                        rect_off[(hh, s4)] = ro
                        p.W("gp", "te")
                        p.DMA("gp", ARH[:, BR_D + ro:BR_D + ro + wid],
                              r2d.ap()[2 * pr + hh, lq0:lq0 + 128,
                                       lq0 + 256:lq0 + 256 + wid], "gps")
                        ro += wid
                p.W("te", "syl")
                p.W("te", "gpl")
                p.W("te", "gps")
                p.W("te", "ve")
                for kg in range(S // 256):
                    for kci in range(2):
                        kc = kg * 2 + kci
                        band_subs = []
                        for s4 in range(4):
                            lq0 = qt * 512 + s4 * 128
                            if kc * 128 >= lq0 + T and (0, s4) in rect_off:
                                band_subs.append(s4)
                        for hh, b0 in ((0, 0), (1, 2)):
                            B = hh * 64
                            last_te = (hh == 1 and kci == 1 and not band_subs)
                            p.OP("te", lambda E, B=B, kc=kc, kci=kci, b0=b0, pr=pr, qt=qt, bs=bool(band_subs):
                                 E.matmul(ps[0:128, (b0 + kci) * 512:(b0 + kci + 1) * 512],
                                          ARH[B:B + 64, BK_S + kc * 128:BK_S + (kc + 1) * 128],
                                          AR1[B:B + 64, pr * T + qt * 512:pr * T + (qt + 1) * 512],
                                          start=True, stop=not bs),
                                 sem="te" if last_te else None)
                        for hh in range(2):
                            b0 = hh * 2
                            for si, s4 in enumerate(band_subs):
                                lq0 = qt * 512 + s4 * 128
                                kcr = (kc * 128 - (lq0 + T)) // 128
                                ro = rect_off[(hh, s4)]
                                lastb = (si == len(band_subs) - 1)
                                last_te = (hh == 1 and kci == 1 and lastb)
                                p.OP("te", lambda E, s4=s4, kci=kci, b0=b0, ro=ro, kcr=kcr, lastb=lastb:
                                     E.matmul(ps[0:128, (b0 + kci) * 512 + s4 * 128:
                                                 (b0 + kci) * 512 + (s4 + 1) * 128],
                                              ARH[:, BR_D + ro + kcr * 128:BR_D + ro + (kcr + 1) * 128],
                                              IDN[:, :], start=False, stop=lastb),
                                     sem="te" if last_te else None)
                    for hh, b0 in ((0, 0), (1, 2)):
                        p.W("sc", "te")
                        p.OP("sc", lambda E, hh=hh, b0=b0, kg=kg: E.activation(
                            ARH[:, BE_X + hh * 2048 + (kg % 2) * 1024:
                                BE_X + hh * 2048 + (kg % 2) * 1024 + 1024],
                            ps[0:128, b0 * 512:(b0 + 2) * 512], AF.Exp, scale=0.125))
                    p.W("te", "sc")
                    for kci in range(2):
                        kc = kg * 2 + kci
                        for hh in range(2):
                            p.OP("te", lambda E, hh=hh, kc=kc, kci=kci, kg=kg: E.matmul(
                                ps[0:65, (4 + hh) * 512:(4 + hh) * 512 + 512],
                                vpk(kc, hh),
                                ARH[:, BE_X + hh * 2048 + (kg % 2) * 1024 + kci * 512:
                                    BE_X + hh * 2048 + (kg % 2) * 1024 + (kci + 1) * 512],
                                start=(kc == 0), stop=(kc == NKC - 1)),
                                sem="te" if (kc == NKC - 1 and hh == 1) else None)
                for hh in range(2):
                    B = hh * 64
                    bv4 = 4 + hh
                    p.W("ve", "te")
                    p.OP("ve", lambda E, bv4=bv4: E.reciprocal(
                        SCRF[64:65, TAc:TAc + 512], ps[64:65, bv4 * 512:bv4 * 512 + 512]))
                    p.OP("ve", lambda E: E.tensor_copy(
                        TMPB.bitcast(BF16)[64:65, 0:512], SCRF[64:65, TAc:TAc + 512]))
                    p.W("te", "ve")
                    p.OP("te", lambda E: E.matmul(
                        ps[0:128, 6 * 512:7 * 512], ONES[64:65, :],
                        TMPB.bitcast(BF16)[64:65, 0:512], start=True, stop=True),
                        sem="te")
                    p.W("sc", "te")
                    p.OP("sc", lambda E, bv4=bv4, B=B: E.activation(
                        TMPA.bitcast(BF16)[0:64, 0:512],
                        ps[0:64, bv4 * 512:bv4 * 512 + 512], AF.Copy))
                    if hh == 0:
                        p.W("ve", "sc")
                        p.OP("ve", lambda E, pr=pr: E.tensor_tensor(
                            att(pr)[0:64, :], TMPA.bitcast(BF16)[0:64, 0:512],
                            ps[0:64, 6 * 512:6 * 512 + 512], ALU.mult))
                    else:
                        p.W("gp", "sc")
                        p.DMA("gp", TMPB.bitcast(BF16)[64:128, 512:1024],
                              TMPA.bitcast(BF16)[0:64, 0:512], "gps")
                        p.W("ve", "gps")
                        p.OP("ve", lambda E, pr=pr: E.tensor_tensor(
                            att(pr)[64:128, :], TMPB.bitcast(BF16)[64:128, 512:1024],
                            ps[64:128, 6 * 512:7 * 512], ALU.mult))

            if dbg and qt == 0:
                p.W("sy", "ve")
                p.DMA("sy", dbgAtt_o.rearrange("(c p) w -> p c w", p=128),
                      ARH[:, BA_T:BA_T + DC * 512].rearrange("p (c w) -> p c w", w=512),
                      "syo")
            p.W("sy", "ve")
            p.W("sy", "syo")
            p.DMA("sy", SCRF[:, XF:XF + DC * 512].rearrange("p (c w) -> p c w", w=512),
                  _fm3(x2T[:, qt * 512:(qt + 1) * 512]), "syl")
            p.W("ve", "syl")
            evo = {}
            for oc in range(DC):
                b = oc % 2
                p.W("te", "ve")
                if oc >= 2:
                    p.W("te", "sc", evo[oc - 2])
                for c in range(DC):
                    p.OP("te", lambda E, oc=oc, c=c, b=b: E.matmul(
                        bank(b),
                        ARH[:, BW_Q + c * D + oc * 128:BW_Q + c * D + (oc + 1) * 128],
                        att(c), start=(c == 0), stop=(c == DC - 1)),
                        sem="te" if c == DC - 1 else None)
                p.W("sc", "te")
                p.OP("sc", lambda E, oc=oc, b=b: E.activation(
                    prj(oc), bank(b), AF.Identity, bias=bcol(OBO, oc)))
                evo[oc] = p.cnt["sc"]

            def cb_x3(c, rb, mb):
                p.OP("ve", lambda E, c=c, rb=rb: E.tensor_tensor(
                    TMPA, prj(c), rb, ALU.mult))
                p.OP("ve", lambda E, c=c, mb=mb: E.tensor_tensor(
                    TMPA, TMPA, mb, ALU.subtract))
                p.OP("ve", lambda E, c=c: E.tensor_tensor(
                    SCRF[:, XF + c * 512:XF + (c + 1) * 512],
                    TMPA, SCRF[:, XF + c * 512:XF + (c + 1) * 512], ALU.add))
                p.OP("ve", lambda E, c=c: E.tensor_copy(
                    x3b(c), SCRF[:, XF + c * 512:XF + (c + 1) * 512]))

            emit_ln(p, cfg, 512, lambda c: prj(c), cb_x3,
                    bank(0)[0:1, :], bank(1)[0:1, :], bank(2), bank(3),
                    ONES, R0, R1, R2r, RB2, lambda c: sqb(c))
            p.W("sy", "ve")
            p.DMA("sy", xsp[:, qt * 512:(qt + 1) * 512].rearrange("(c p) w -> p c w", p=128),
                  SCRF[:, XF:XF + DC * 512].rearrange("p (c w) -> p c w", w=512), "syo")

            def cb_xn3(c, rb, mb, qt=qt):
                p.OP("ve", lambda E, c=c, rb=rb: E.tensor_tensor(
                    TMPA, x3b(c), rb, ALU.mult))
                p.OP("ve", lambda E, c=c, mb=mb, qt=qt: E.tensor_tensor(
                    AR1[:, c * T + qt * 512:c * T + (qt + 1) * 512],
                    TMPA, mb, ALU.subtract))

            emit_ln(p, cfg, 512, lambda c: x3b(c), cb_xn3,
                    bank(0)[0:1, :], bank(1)[0:1, :], bank(2), bank(3),
                    ONES, R0, R1, R2r, RB2, lambda c: sqb(c))

        if stop_after == "B4":
            p.W("sy", "syo")
            p.emit(block)
            return nc

        hslot = lambda o, ts: ARH[:, o * T + ts * 512:o * T + (ts + 1) * 512]
        ev1 = {}
        git = 0
        for og in range(DH // 256):
            p.W("gp", "te")
            p.W("gp", "sc")
            p.DMA("gp", WBUF[:, 0:DC * 256].rearrange("p (c w) -> p c w", w=256),
                  _fm3(fw1T[:, og * 256:(og + 1) * 256]), "gpl")
            p.DMA("gp", WBUF[:, DC * 256:2 * DC * 256].rearrange("p (c w) -> p c w", w=256),
                  _fm3(fw2T[:, og * 256:(og + 1) * 256]), "gpl")
            p.W("te", "gpl")
            for o4 in range(2):
                o = og * 2 + o4
                for ts in range(NT):
                    it = git
                    git += 1
                    pa = (it % 2) * 2
                    if it >= 2:
                        p.W("te", "ve", ev1[it - 2])
                    elif it == 0:
                        p.W("te", "ve")
                        p.W("te", "sc")
                    for c in range(DC):
                        p.OP("te", lambda E, c=c, o4=o4, ts=ts, pa=pa: E.matmul(
                            bank(pa),
                            WBUF[:, c * 256 + o4 * 128:c * 256 + (o4 + 1) * 128],
                            AR1[:, c * T + ts * 512:c * T + (ts + 1) * 512],
                            start=(c == 0), stop=(c == DC - 1)))
                    for c in range(DC):
                        p.OP("te", lambda E, c=c, o4=o4, ts=ts, pa=pa: E.matmul(
                            bank(pa + 1),
                            WBUF[:, DC * 256 + c * 256 + o4 * 128:
                                 DC * 256 + c * 256 + (o4 + 1) * 128],
                            AR1[:, c * T + ts * 512:c * T + (ts + 1) * 512],
                            start=(c == 0), stop=(c == DC - 1)),
                            sem="te" if c == DC - 1 else None)
                    p.W("sc", "te")
                    p.OP("sc", lambda E, o=o, ts=ts, pa=pa: E.activation(
                        hslot(o, ts), bank(pa), AF.Silu, scale=bcol(OS1, o)))
                    p.W("ve", "sc")
                    p.OP("ve", lambda E, o=o, ts=ts, pa=pa: E.scalar_tensor_tensor(
                        hslot(o, ts), bank(pa + 1), bcol(OS2, o), hslot(o, ts),
                        ALU.mult, ALU.mult))
                    ev1[it] = p.cnt["ve"]

        def ffn_mid(wsrc, scol_off, out_cb_o):
            p.W("te", "ve")
            p.W("te", "sc")
            for o2 in range(DH // 128):
                p.W("gp", "te")
                p.DMA("gp", WBUF[:, 0:HC * 128].rearrange("p (c w) -> p c w", w=128),
                      wsrc[:, o2 * 128:(o2 + 1) * 128].rearrange("(c p) w -> p c w", p=128),
                      "gpl")
                p.W("te", "gpl")
                for ts in range(NT):
                    b = ts % 2
                    p.W("te", "sc")
                    for c in range(HC):
                        p.OP("te", lambda E, c=c, ts=ts, b=b: E.matmul(
                            bank(b),
                            WBUF[:, c * 128:(c + 1) * 128],
                            ARH[:, c * T + ts * 512:c * T + (ts + 1) * 512],
                            start=(c == 0), stop=(c == HC - 1)),
                            sem="te" if c == HC - 1 else None)
                    p.W("sc", "te")
                    out_cb_o(o2, ts, b, scol_off)

        gstage = lambda o2: AR1[:, (o2 % 2) * T:(o2 % 2) * T + T]

        def g_out(o2, ts, b, scol_off):
            if ts == 0:
                p.W("sc", "gps")
            p.OP("sc", lambda E, o2=o2, ts=ts, b=b: E.activation(
                gstage(o2)[:, ts * 512:(ts + 1) * 512], bank(b), AF.Gelu,
                scale=bcol(scol_off, o2)))
            if ts == NT - 1:
                p.W("gp", "sc")
                p.DMA("gp", g_sp[o2], gstage(o2), "gps")

        ffn_mid(fh1T, OSH1, g_out)

        p.W("sy", "gps")
        p.W("sy", "te")
        p.DMA("sy", ARH[:, 0:HC * T].rearrange("p (c w) -> p c w", w=T),
              g_sp.rearrange("c p w -> p c w"), "syl")
        p.W("te", "syl")

        def g2_out(o3, ts, b, scol_off):
            slot = (o3 * NT + ts) % 2
            p.W("sc", "gps")
            p.OP("sc", lambda E, o3=o3, ts=ts, b=b, slot=slot: E.activation(
                AR1[:, slot * 512:(slot + 1) * 512], bank(b), AF.Gelu,
                scale=bcol(scol_off, o3)))
            p.W("gp", "sc")
            p.DMA("gp", g2_sp[ts, o3], AR1[:, slot * 512:(slot + 1) * 512], "gps")

        ffn_mid(fh2T, OSH2, g2_out)

        W3_0 = 0
        B9S = HC * D
        ybf = lambda oc: WBUF[:, oc * 512:(oc + 1) * 512]
        x4b = lambda c: ARH[:, B9S + c * 512:B9S + (c + 1) * 512]
        sq9 = lambda c: ARH[:, B9S + DC * 512 + c * 512:B9S + DC * 512 + (c + 1) * 512]
        p.W("gp", "te")
        p.W("gp", "sc")
        p.DMA("gp", ARH[:, W3_0:W3_0 + HC * D].rearrange("p (c w) -> p c w", w=D),
              fw3T.rearrange("(c p) w -> p c w", p=128), "gpl")
        for ts in range(NT):
            p.W("sy", "gps")
            p.W("sy", "syo")
            p.W("sy", "te")
            p.W("sy", "ve")
            p.DMA("sy", AR1[:, 0:HC * 512].rearrange("p (c w) -> p c w", w=512),
                  g2_sp[ts].rearrange("c p w -> p c w"), "syl")
            p.DMA("sy", SCRF[:, XF:XF + DC * 512].rearrange("p (c w) -> p c w", w=512),
                  xsp[:, ts * 512:(ts + 1) * 512].rearrange("(c p) w -> p c w", p=128),
                  "syl")
            p.W("te", "syl")
            p.W("te", "gpl")
            p.W("te", "ve")
            p.W("te", "sc")
            for oc in range(DC):
                for c in range(HC):
                    p.OP("te", lambda E, oc=oc, c=c: E.matmul(
                        bank(oc),
                        ARH[:, W3_0 + c * D + oc * 128:W3_0 + c * D + (oc + 1) * 128],
                        AR1[:, c * 512:(c + 1) * 512],
                        start=(c == 0), stop=(c == HC - 1)),
                        sem="te" if (c == HC - 1 and oc == DC - 1) else None)
            p.W("sc", "te")
            for oc in range(DC):
                p.OP("sc", lambda E, oc=oc: E.activation(
                    ybf(oc), bank(oc), AF.Copy, scale=bcol(OS3, oc)))

            def cb_x4(c, rb, mb):
                p.OP("ve", lambda E, c=c, rb=rb: E.tensor_tensor(
                    TMPA, ybf(c), rb, ALU.mult))
                p.OP("ve", lambda E, c=c, mb=mb: E.tensor_tensor(
                    TMPA, TMPA, mb, ALU.subtract))
                p.OP("ve", lambda E, c=c: E.tensor_tensor(
                    SCRF[:, XF + c * 512:XF + (c + 1) * 512],
                    TMPA, SCRF[:, XF + c * 512:XF + (c + 1) * 512], ALU.add))

            emit_ln(p, cfg, 512, lambda c: ybf(c), cb_x4,
                    bank(0)[0:1, :], bank(1)[0:1, :], bank(2), bank(3),
                    ONES, R0, R1, R2r, RB2, sq9)
            p.W("sy", "ve")
            p.DMA("sy", x4T_o[:, ts * 512:(ts + 1) * 512].rearrange("(c p) w -> p c w", p=128),
                  SCRF[:, XF:XF + DC * 512].rearrange("p (c w) -> p c w", w=512), "syo")

        if dbg:
            p.W("sy", "syo")
            p.DMA("sy", dbgX3_o, xsp, "syo")
        p.W("sy", "syo")
        p.W("sy", "gps")
        p.emit(block)
    return nc


def _fm_vec(v, nch):
    return np.ascontiguousarray(np.asarray(v, np.float32).reshape(nch, 128).T)


def _t(a):
    return np.ascontiguousarray(np.asarray(a, np.float32).T)


def kernel(**inp):
    cfg = Cfg()
    D, T, S, DC, HC, H = cfg.D, cfg.T, cfg.S, cfg.DC, cfg.HC, cfg.H
    ML = T

    WKV = D // 2
    DCq = WKV // 128
    Wqkv = np.asarray(inp["L0_attn_W_qkv"], np.float32)
    bqkv = np.asarray(inp["L0_attn_b_qkv"], np.float32)
    sharedA = {
        "wprojT": _t(inp["L0_attn_W_proj"]),
        "bprojF": _fm_vec(inp["L0_attn_b_proj"], DC),
        "gammaF": _fm_vec(inp["L0_attn_gamma"], DC),
        "betaF": _fm_vec(inp["L0_attn_beta"], DC),
        "w1T": _t(inp["L0_ffn_w1_w"]),
        "w2T": _t(inp["L0_ffn_w2_w"]),
        "h1T": _t(inp["L0_ffn_h1_w"]),
        "h2T": _t(inp["L0_ffn_h2_w"]),
        "w3T": _t(inp["L0_ffn_w3_w"]),
        "s1F": _fm_vec(inp["L0_ffn_w1_s"], HC),
        "s2F": _fm_vec(inp["L0_ffn_w2_s"], HC),
        "sh1F": _fm_vec(inp["L0_ffn_h1_s"], HC),
        "sh2F": _fm_vec(inp["L0_ffn_h2_s"], HC),
        "s3F": _fm_vec(inp["L0_ffn_w3_s"], DC),
        "wkT": _t(inp["L1_attn_Wk"]),
        "wvT": _t(inp["L1_attn_Wv"]),
        "bkF": _fm_vec(inp["L1_attn_bk"], DC),
        "bvrow": np.ascontiguousarray(
            np.asarray(inp["L1_attn_bv"], np.float32)).reshape(1, D),
    }
    x = np.asarray(inp["L0_attn_H"], np.float32)
    hgA = []
    for hf in range(2):
        hgA.append({
            "wqgT": np.ascontiguousarray(Wqkv[WKV * hf:WKV * (hf + 1)].T),
            "wkvT": np.ascontiguousarray(np.concatenate(
                [Wqkv[D + WKV * hf:D + WKV * (hf + 1)],
                 Wqkv[2 * D + WKV * hf:2 * D + WKV * (hf + 1)]], 0).T),
            "bqF": _fm_vec(bqkv[WKV * hf:WKV * (hf + 1)], DCq),
            "bkvrow": np.ascontiguousarray(np.concatenate(
                [bqkv[D + WKV * hf:D + WKV * (hf + 1)],
                 bqkv[2 * D + WKV * hf:2 * D + WKV * (hf + 1)]])).reshape(1, D),
        })
    mapsA = []
    for c in range(8):
        b, hf = c // 2, c % 2
        own = x[b, hf * T:(hf + 1) * T]
        sib = x[b, (1 - hf) * T:(2 - hf) * T]
        m = {"x0T": np.ascontiguousarray(np.concatenate([own, sib], 0).T),
             "x0Tg": np.ascontiguousarray(x[b].T)}
        m.update(sharedA)
        m.update(hgA[hf])
        mapsA.append(m)

    ncA = build_launch_A(cfg)
    resA = run_bass_kernel_spmd(ncA, mapsA, core_ids=list(range(8))).results

    Er = np.asarray(inp["L1_attn_Er"], np.float32)
    sharedB = {
        "wqT": _t(inp["L1_attn_Wq"]),
        "bqF": _fm_vec(inp["L1_attn_bq"], DC),
        "woT": _t(inp["L1_attn_Wo"]),
        "boF": _fm_vec(inp["L1_attn_bo"], DC),
        "idn": np.eye(128, dtype=ml_dtypes.bfloat16),
        "fw1T": _t(inp["L1_ffn_w1_w"]),
        "fw2T": _t(inp["L1_ffn_w2_w"]),
        "fh1T": _t(inp["L1_ffn_h1_w"]),
        "fh2T": _t(inp["L1_ffn_h2_w"]),
        "fw3T": _t(inp["L1_ffn_w3_w"]),
        "fs1F": _fm_vec(inp["L1_ffn_w1_s"], HC),
        "fs2F": _fm_vec(inp["L1_ffn_w2_s"], HC),
        "fsh1F": _fm_vec(inp["L1_ffn_h1_s"], HC),
        "fsh2F": _fm_vec(inp["L1_ffn_h2_s"], HC),
        "fs3F": _fm_vec(inp["L1_ffn_w3_s"], DC),
    }
    erxs = []
    for hf in range(2):
        o = hf * T
        erx = np.empty((ML + 1, 64), np.float32)
        for cc in range(ML):
            erx[cc] = Er[min(ML - 1 - cc + o, ML - 1)]
        erx[ML] = Er[ML - 1]
        erxs.append(np.ascontiguousarray(erx.T))

    mapsB = []
    for c in range(8):
        b, hf = c // 2, c % 2
        sibc = c ^ 1
        m = {
            "x2T": resA[c]["x2T"],
            "xn2T": resA[c]["xn2T"],
            "KTp": np.ascontiguousarray(
                np.concatenate([resA[c]["KT"], resA[sibc]["KT"]], axis=1)),
            "Vp": np.ascontiguousarray(
                np.concatenate([resA[c]["V"], resA[sibc]["V"]], axis=0)),
            "erxT": erxs[hf],
        }
        m.update(sharedB)
        mapsB.append(m)

    ncB = build_launch_B(cfg)
    resB = run_bass_kernel_spmd(ncB, mapsB, core_ids=list(range(8))).results

    out = np.empty((4, S, D), np.float32)
    for c in range(8):
        b, hf = c // 2, c % 2
        out[b, hf * T:(hf + 1) * T] = resB[c]["x4T"].T
    return out


# version 08 (final)
# speedup vs baseline: 1.2083x; 1.2083x over previous
import numpy as np
import ml_dtypes
from contextlib import ExitStack
import concourse.bass as bass
import concourse.mybir as mybir
from concourse.bass_utils import run_bass_kernel_spmd

F32 = mybir.dt.float32
BF16 = mybir.dt.bfloat16
AF = mybir.ActivationFunctionType
ALU = mybir.AluOpType
BF = ml_dtypes.bfloat16

LN_EPS = 1e-5


class Cfg:
    def __init__(self, D=1024, T=2048, DH=4096):
        self.D = D
        self.T = T
        self.S = 2 * T
        self.DK = 64
        self.H = D // 64
        self.DH = DH
        self.DC = D // 128
        self.HC = DH // 128
        self.NT = T // 512
        self.NTS = 2 * T // 512


class P:

    ENGS = ("te", "ve", "sc", "gp", "sy")

    def __init__(self, nc, ctx):
        self.nc = nc
        self.ctx = ctx
        self._sems = {}
        self.cnt = {}
        self.ops = {e: [] for e in self.ENGS}

    def sem(self, name):
        if name not in self._sems:
            self._sems[name] = self.ctx.enter_context(self.nc.semaphore(name))
            self.cnt[name] = 0
        return self._sems[name]

    def W(self, eng, semname, val=None):
        s = self.sem(semname)
        v = self.cnt[semname] if val is None else val
        if v > 0:
            self.ops[eng].append(lambda E, s=s, v=v: E.wait_ge(s, v))

    def OP(self, eng, fn, sem=None, n=1):
        if eng in ("ve", "sc"):
            s = self.sem(eng)
            v = self.cnt[eng]
            if v > 0:
                self.ops[eng].append(lambda E, s=s, v=v: E.wait_ge(s, v))
            self.cnt[eng] += 1

            def run(E, fn=fn, s=s):
                fn(E).then_inc(s, 1)
            self.ops[eng].append(run)
            return self.cnt[eng]
        if sem is not None:
            s = self.sem(sem)
            self.cnt[sem] += n

            def run2(E, fn=fn, s=s, n=n):
                fn(E).then_inc(s, n)
            self.ops[eng].append(run2)
        else:
            self.ops[eng].append(lambda E, fn=fn: fn(E))
        return self.cnt.get(sem, 0)

    def DMA(self, eng, out, in_, sem):
        s = self.sem(sem)
        self.cnt[sem] += 16
        self.ops[eng].append(
            lambda E, o=out, i=in_, s=s: E.dma_start(out=o, in_=i).then_inc(s, 16))
        return self.cnt[sem]

    def emit(self, block):
        @block.tensor
        def _(E):
            for f in self.ops["te"]:
                f(E)

        @block.vector
        def _(E):
            for f in self.ops["ve"]:
                f(E)

        @block.scalar
        def _(E):
            for f in self.ops["sc"]:
                f(E)

        @block.gpsimd
        def _(E):
            for f in self.ops["gp"]:
                f(E)

        @block.sync
        def _(E):
            for f in self.ops["sy"]:
                f(E)


def emit_ln(p, cfg, W, src_bf, out_cb, st0, st1, rb, mb, ONES, R0, R1, R2r,
            RB2, SQ_of):
    DC, D = cfg.DC, cfg.D
    p.W("sc", "ve")
    p.W("sc", "te")
    for c in range(DC):
        p.OP("sc", lambda E, c=c: E.activation(SQ_of(c), src_bf(c), AF.Square),
             sem="sc" if c == DC - 1 else None)
    p.W("te", "sc")
    for c in range(DC):
        p.OP("te", lambda E, c=c: E.matmul(st0, ONES[:, 0:1], src_bf(c),
                                           start=(c == 0), stop=(c == DC - 1)))
    for c in range(DC):
        p.OP("te", lambda E, c=c: E.matmul(st1, ONES[:, 0:1], SQ_of(c),
                                           start=(c == 0), stop=(c == DC - 1)),
             sem="te" if c == DC - 1 else None)
    p.W("ve", "te")
    p.OP("ve", lambda E: E.tensor_scalar_mul(R0, st0, 1.0 / D))
    p.OP("ve", lambda E: E.tensor_scalar_mul(R1, st1, 1.0 / D))
    p.OP("ve", lambda E: E.tensor_tensor(R2r, R0, R0, ALU.mult))
    p.OP("ve", lambda E: E.tensor_tensor(R1, R1, R2r, ALU.subtract))
    p.OP("ve", lambda E: E.tensor_scalar_add(R1, R1, LN_EPS))
    p.OP("ve", lambda E: E.reciprocal(R1, R1), sem="ve")
    p.W("sc", "ve")
    p.OP("sc", lambda E: E.activation(R1, R1, AF.Sqrt), sem="sc")
    p.W("ve", "sc")
    p.OP("ve", lambda E: E.tensor_tensor(R0, R0, R1, ALU.mult))
    p.OP("ve", lambda E: E.tensor_copy(RB2[0:1, 0:W], R1))
    p.OP("ve", lambda E: E.tensor_copy(RB2[0:1, W:2 * W], R0), sem="ve")
    p.W("te", "ve")
    p.OP("te", lambda E: E.matmul(rb, ONES[0:1, :], RB2[0:1, 0:W],
                                  start=True, stop=True))
    p.OP("te", lambda E: E.matmul(mb, ONES[0:1, :], RB2[0:1, W:2 * W],
                                  start=True, stop=True), sem="te")
    p.W("ve", "te")
    for c in range(DC):
        out_cb(c, rb, mb)


def _fm3(hbm_2d, p128=128, w=None):
    return hbm_2d.rearrange("(c p) w -> p c w", p=p128)



def emit_ffn_stages(p, cfg, ARH, AR1, WBUF, SCRF, ps, bcol, OS1, OS2, OSH1,
                    OSH2, w1T, w2T, h1T, h2T, g_sp, g2_sp):
    D, T, DC, HC, NT, DH = cfg.D, cfg.T, cfg.DC, cfg.HC, cfg.NT, cfg.DH
    bank = lambda i: ps[:, i * 512:(i + 1) * 512]
    SBF = SCRF[:, :].bitcast(BF16)
    hslot = lambda o, ts: ARH[:, o * T + ts * 512:o * T + (ts + 1) * 512]
    SLAB1 = 2 * DC * 256
    SL23 = HC * 128
    SLB = 2 * T
    gw = ("gw0", "gw1")

    def sl23(s, c):
        return WBUF[:, c * 128:(c + 1) * 128] if s == 0 else \
            AR1[:, SLB + c * 128:SLB + (c + 1) * 128]

    def sl23f(s):
        return WBUF[:, 0:SL23] if s == 0 else AR1[:, SLB:SLB + SL23]

    p.W("gp", "ve")
    p.W("gp", "te")
    p.W("gp", "syo")
    gw_after, te_og, ev1 = {}, {}, {}

    def ld1(og):
        s = og % 2
        p.DMA("gp", SBF[:, s * SLAB1:s * SLAB1 + DC * 256].rearrange(
            "p (c w) -> p c w", w=256),
            _fm3(w1T[:, og * 256:(og + 1) * 256]), gw[s])
        p.DMA("gp", SBF[:, s * SLAB1 + DC * 256:(s + 1) * SLAB1].rearrange(
            "p (c w) -> p c w", w=256),
            _fm3(w2T[:, og * 256:(og + 1) * 256]), gw[s])
        gw_after[og] = p.cnt[gw[s]]

    git = 0
    NOG = DH // 256
    for og in range(NOG):
        if og == 0:
            ld1(0)
            if NOG > 1:
                ld1(1)
        s = og % 2
        p.W("te", gw[s], gw_after[og])
        for o4 in range(2):
            o = og * 2 + o4
            for ts in range(NT):
                it = git
                git += 1
                pa = (it % 2) * 2
                if it >= 2:
                    p.W("te", "ve", ev1[it - 2])
                elif it == 0:
                    p.W("te", "ve")
                    p.W("te", "sc")
                for c in range(DC):
                    p.OP("te", lambda E, c=c, s=s, o4=o4, ts=ts, pa=pa: E.matmul(
                        bank(pa),
                        SBF[:, s * SLAB1 + c * 256 + o4 * 128:
                            s * SLAB1 + c * 256 + (o4 + 1) * 128],
                        AR1[:, c * T + ts * 512:c * T + (ts + 1) * 512],
                        start=(c == 0), stop=(c == DC - 1)))
                for c in range(DC):
                    p.OP("te", lambda E, c=c, s=s, o4=o4, ts=ts, pa=pa: E.matmul(
                        bank(pa + 1),
                        SBF[:, s * SLAB1 + DC * 256 + c * 256 + o4 * 128:
                            s * SLAB1 + DC * 256 + c * 256 + (o4 + 1) * 128],
                        AR1[:, c * T + ts * 512:c * T + (ts + 1) * 512],
                        start=(c == 0), stop=(c == DC - 1)),
                        sem="te" if c == DC - 1 else None)
                p.W("sc", "te")
                p.OP("sc", lambda E, o=o, ts=ts, pa=pa: E.activation(
                    hslot(o, ts), bank(pa), AF.Silu, scale=bcol(OS1, o)))
                p.W("ve", "sc")
                p.OP("ve", lambda E, o=o, ts=ts, pa=pa: E.scalar_tensor_tensor(
                    hslot(o, ts), bank(pa + 1), bcol(OS2, o), hslot(o, ts),
                    ALU.mult, ALU.mult))
                ev1[it] = p.cnt["ve"]
        te_og[og] = p.cnt["te"]
        if og + 2 < NOG:
            p.W("gp", "te", te_og[og])
            ld1(og + 2)

    p.DMA("gp", sl23f(0).rearrange("p (c w) -> p c w", w=128),
          h1T[:, 0:128].rearrange("(c p) w -> p c w", p=128), "gw2")
    pre0 = p.cnt["gw2"]

    NO2 = DH // 128

    def mid(wsrc, scol_off, out_cb_o, pre0=None):
        p.W("te", "ve")
        p.W("te", "sc")
        gwa, tea, gelu_after = {}, {}, {}

        def ld(o2, gate):
            s = o2 % 2
            p.W("gp", "te", gate)
            p.DMA("gp", sl23f(s).rearrange("p (c w) -> p c w", w=128),
                  wsrc[:, o2 * 128:(o2 + 1) * 128].rearrange(
                      "(c p) w -> p c w", p=128), gw[s])
            gwa[o2] = p.cnt[gw[s]]

        for o2 in range(NO2):
            if o2 == 0:
                if pre0 is None:
                    ld(0, p.cnt["te"])
                if NO2 > 1:
                    ld(1, p.cnt["te"])
            if o2 == 0 and pre0 is not None:
                p.W("te", "gw2", pre0)
            else:
                p.W("te", gw[o2 % 2], gwa[o2])
            for ts in range(NT):
                b = ts % 2
                if (o2 - 1, ts) in gelu_after:
                    p.W("te", "sc", gelu_after[(o2 - 1, ts)])
                for c in range(HC):
                    p.OP("te", lambda E, c=c, o2=o2, ts=ts, b=b: E.matmul(
                        bank(b), sl23(o2 % 2, c),
                        ARH[:, c * T + ts * 512:c * T + (ts + 1) * 512],
                        start=(c == 0), stop=(c == HC - 1)),
                        sem="te" if c == HC - 1 else None)
                p.W("sc", "te", p.cnt["te"])
                out_cb_o(o2, ts, b, scol_off)
                gelu_after[(o2, ts)] = p.cnt["sc"]
            tea[o2] = p.cnt["te"]
            if o2 + 2 < NO2:
                ld(o2 + 2, tea[o2])

    gstage = lambda o2: AR1[:, (o2 % 2) * T:(o2 % 2) * T + T]

    def g_out(o2, ts, b, scol_off):
        if ts == 0:
            p.W("sc", "sps")
        p.OP("sc", lambda E, o2=o2, ts=ts, b=b: E.activation(
            gstage(o2)[:, ts * 512:(ts + 1) * 512], bank(b), AF.Gelu,
            scale=bcol(scol_off, o2)))
        if ts == NT - 1:
            p.W("sy", "sc")
            p.DMA("sy", g_sp[o2], gstage(o2), "sps")

    mid(h1T, OSH1, g_out, pre0=pre0)

    HH = HC // 2
    p.W("gp", "sps")
    p.W("gp", "te")
    p.DMA("gp", ARH[:, 0:HH * T].rearrange("p (c w) -> p c w", w=T),
          g_sp[0:HH].rearrange("c p w -> p c w"), "scl")
    p.W("sy", "sps")
    p.W("sy", "te")
    p.DMA("sy", ARH[:, HH * T:HC * T].rearrange("p (c w) -> p c w", w=T),
          g_sp[HH:HC].rearrange("c p w -> p c w"), "syl")
    p.W("te", "scl")
    p.W("te", "syl")

    def g2_out(o3, ts, b, scol_off):
        slot = o3 % 2
        base = slot * NT * 512
        if ts == 0:
            p.W("sc", "sps")
        p.OP("sc", lambda E, o3=o3, ts=ts, b=b, base=base: E.activation(
            AR1[:, base + ts * 512:base + (ts + 1) * 512], bank(b), AF.Gelu,
            scale=bcol(scol_off, o3)))
        if ts == NT - 1:
            p.W("sy", "sc")
            p.DMA("sy", g2_sp[:, o3].rearrange("t p w -> p t w"),
                  AR1[:, base:base + NT * 512].rearrange(
                      "p (t w) -> p t w", w=512), "sps")

    mid(h2T, OSH2, g2_out)


def build_launch_A(cfg, stop_after=None, dbg=False):
    D, T, S, DK, H, DH = cfg.D, cfg.T, cfg.S, cfg.DK, cfg.H, cfg.DH
    DC, HC, NT, NTS = cfg.DC, cfg.HC, cfg.NT, cfg.NTS
    HL = H // 2
    WKV = D // 2
    DCq = WKV // 128
    H65 = HL * 65
    JR = D // 64
    MJ = 512 // JR
    RPH = T // HL
    NQ = max(1, D // 512)
    nc = bass.Bass("TRN2", target_bir_lowering=False)

    din = lambda n, s, d=F32: nc.dram_tensor(n, s, d, kind="ExternalInput").ap()
    dout = lambda n, s, d: nc.dram_tensor(n, s, d, kind="ExternalOutput").ap()

    x0T = din("x0T", [D, S])
    x0Tg = din("x0Tg", [D, S])
    wqgT = din("wqgT", [D, WKV])
    wkvT = din("wkvT", [D, D])
    bqF = din("bqF", [128, DCq])
    bkvrow = din("bkvrow", [1, D])
    wprojT = din("wprojT", [D, D])
    bprojF = din("bprojF", [128, DC])
    gammaF = din("gammaF", [128, DC])
    betaF = din("betaF", [128, DC])
    w1T = din("w1T", [D, DH])
    w2T = din("w2T", [D, DH])
    h1T = din("h1T", [DH, DH])
    h2T = din("h2T", [DH, DH])
    w3T = din("w3T", [DH, D])
    s1F = din("s1F", [128, HC])
    s2F = din("s2F", [128, HC])
    sh1F = din("sh1F", [128, HC])
    sh2F = din("sh2F", [128, HC])
    s3F = din("s3F", [128, DC])
    wkT = din("wkT", [D, D])
    wvT = din("wvT", [D, D])
    bkF = din("bkF", [128, DC])
    bvrow = din("bvrow", [1, D])

    x2T_o = dout("x2T", [D, T], F32)
    xn2T_o = dout("xn2T", [D, T], BF16)
    KT_o = dout("KT", [D, T], BF16)
    V_o = dout("V", [T, D], BF16)

    dbgV_o = dout("dbgV", [128, D], BF16) if dbg else None
    dbgV2_o = dout("dbgV2", [128, D], BF16) if dbg else None
    scr_d = nc.dram_tensor("scr_d", [D, T], BF16, kind="Internal").ap()
    xsp = nc.dram_tensor("xsp", [D, T], F32, kind="Internal").ap()
    g_sp = nc.dram_tensor("g_sp", [HC, 128, T], BF16, kind="Internal").ap()
    g2_sp = nc.dram_tensor("g2_sp", [NT, HC, 128, 512], BF16, kind="Internal").ap()

    A2_END = DC * (D + WKV + 3 * D) + DC * 512 + max(DC * 512, 4 * H65 + 64) \
        + DC * 512 + D + DCq * 512 + 1024
    A9_END = HC * D + 2 * DC * D + D + 3 * (DC * 512) + 512
    ARH_COLS = max(HC * T, A2_END, A9_END)
    AR1_COLS = max(DC * T, HC * 512, 2 * T)

    with ExitStack() as ctx:
        sbt = lambda n, s, d: ctx.enter_context(nc.sbuf_tensor(n, s, d))
        AR1 = sbt("AR1", [128, AR1_COLS], BF16)
        ARH = sbt("ARH", [128, ARH_COLS], BF16)
        WBUF = sbt("WBUF", [128, max(2 * DC * 256, HC * 128, DC * 512)], BF16)
        SCRF = sbt("SCRF", [128, DC * 512 + 512 * 2 + 1536], F32)
        KVS2 = sbt("KVS2", [128, H65], BF16)
        KVA = sbt("KVA", [64, H65], F32)
        ONES = sbt("ONES", [128, 128], BF16)
        BIAS = sbt("BIAS", [128, 16 * DC + 4 * HC], F32)
        RB2 = sbt("RB2", [1, 1024], BF16)
        BVR = sbt("BVR", [1, D], BF16)
        ps = ctx.enter_context(nc.psum_tensor("ps", [128, 4096], F32))
        block = ctx.enter_context(nc.Block())
        p = P(nc, ctx)
        for s in ("syl", "syo", "gpl", "gps", "te", "ve", "sc",
                  "gw0", "gw1", "gw2", "scl", "sps"):
            p.sem(s)

        bank = lambda i: ps[:, i * 512:(i + 1) * 512]

        XF = 0
        TAc = DC * 512
        TBc = TAc + 512
        RWc = TBc + 512
        R0 = SCRF[0:1, RWc:RWc + 512]
        R1 = SCRF[0:1, RWc + 512:RWc + 1024]
        R2r = SCRF[0:1, RWc + 1024:RWc + 1536]
        TMPA = SCRF[:, TAc:TAc + 512]
        TMPB = SCRF[:, TBc:TBc + 512]

        OBQ, OBP, OGA, OBE, OS3, OBK = 0, DC, 2 * DC, 3 * DC, 4 * DC, 5 * DC
        OS1 = 16 * DC
        OS2, OSH1, OSH2 = OS1 + HC, OS1 + 2 * HC, OS1 + 3 * HC
        bcol = lambda off, i: BIAS[:, off + i:off + i + 1]

        AW_KV = 0
        AW_Q = AW_KV + DC * D
        AW_P = AW_Q + DC * WKV
        AW_K = AW_P + DC * D
        AW_V = AW_K + DC * D
        A_SCR = AW_V + DC * D
        RS_XB = A_SCR
        RS_SQ = RS_XB + DC * 512
        RS_XNS = RS_SQ + max(DC * 512, 4 * H65 + 64)
        RS_BKV = RS_XNS + DC * 512
        RS_Q4 = RS_BKV + D
        ATT0 = RS_Q4 + DCq * 512
        assert ATT0 + 1024 <= ARH_COLS, (ATT0, ARH_COLS)
        W3_0 = 0
        WK_0 = W3_0 + HC * D
        WV_0 = WK_0 + DC * D
        A9_SCR = WV_0 + DC * D
        BV1_0 = A9_SCR
        X2B_0 = BV1_0 + D
        KB_0 = X2B_0 + DC * 512
        VS_0 = KB_0 + DC * 512
        SQ9_0 = VS_0 + 512
        assert SQ9_0 + DC * 512 <= ARH_COLS

        for dst, src in ((BIAS[:, OBQ:OBQ + DCq], bqF),
                         (BIAS[:, OBP:OBP + DC], bprojF),
                         (BIAS[:, OGA:OGA + DC], gammaF),
                         (BIAS[:, OBE:OBE + DC], betaF),
                         (BIAS[:, OS3:OS3 + DC], s3F),
                         (BIAS[:, OBK:OBK + DC], bkF),
                         (BIAS[:, OS1:OS1 + HC], s1F),
                         (BIAS[:, OS2:OS2 + HC], s2F),
                         (BIAS[:, OSH1:OSH1 + HC], sh1F),
                         (BIAS[:, OSH2:OSH2 + HC], sh2F)):
            p.DMA("sy", dst, src, "syl")
        p.DMA("gp", BVR[0:1, :], bvrow, "gpl")

        def wload(dstbase, src, ncols):
            dst = ARH[:, dstbase:dstbase + DC * ncols].rearrange(
                "p (c w) -> p c w", w=ncols)
            p.DMA("gp", dst, _fm3(src), "gpl")

        wload(AW_KV, wkvT, D)
        wload(AW_Q, wqgT, WKV)
        wload(AW_P, wprojT, D)
        wload(AW_K, wkT, D)
        wload(AW_V, wvT, D)
        p.DMA("gp", ARH[0:1, RS_BKV:RS_BKV + D], bkvrow, "gpl")
        p.OP("ve", lambda E: E.memset(ONES[:, :], 1.0), sem="ve")
        p.W("te", "gpl")
        p.W("te", "ve")
        nsl = D // 512
        for j in range(nsl):
            p.OP("te", lambda E, j=j: E.matmul(
                bank(j % 4), ONES[0:1, :],
                ARH[0:1, RS_BKV + j * 512:RS_BKV + (j + 1) * 512],
                start=True, stop=True), sem="te" if j == nsl - 1 else None)
        p.W("ve", "te")
        for j in range(nsl):
            p.OP("ve", lambda E, j=j: E.tensor_copy(
                ARH[:, RS_BKV + j * 512:RS_BKV + (j + 1) * 512], bank(j % 4)),
                sem="ve" if j == nsl - 1 else None)
        BKV = ARH[:, RS_BKV:RS_BKV + D]

        for tt in range(NTS):
            own = tt < NT

            def xnc(c, tt=tt, own=own):
                if own:
                    return AR1[:, c * T + tt * 512:c * T + tt * 512 + 512]
                return ARH[:, RS_XNS + c * 512:RS_XNS + (c + 1) * 512]

            p.W("sy", "ve")
            p.DMA("sy", SCRF[:, XF:XF + DC * 512].rearrange("p (c w) -> p c w", w=512),
                  _fm3(x0T[:, tt * 512:(tt + 1) * 512]), "syl")
            p.W("gp", "te")
            p.DMA("gp", ARH[:, RS_XB:RS_XB + DC * 512].rearrange("p (c w) -> p c w", w=512),
                  _fm3(x0T[:, tt * 512:(tt + 1) * 512]), "gpl")
            p.W("ve", "syl")
            p.W("sc", "gpl")
            p.W("te", "gpl")

            def mkcb(xnc=xnc):
                def cb(c, rb, mb):
                    p.OP("ve", lambda E, c=c, rb=rb: E.tensor_tensor(
                        TMPA, SCRF[:, XF + c * 512:XF + (c + 1) * 512], rb, ALU.mult))
                    p.OP("ve", lambda E, c=c, mb=mb, xnc=xnc: E.tensor_tensor(
                        xnc(c), TMPA, mb, ALU.subtract),
                        sem="ve" if c == DC - 1 else None)
                return cb

            emit_ln(p, cfg, 512,
                    lambda c: ARH[:, RS_XB + c * 512:RS_XB + (c + 1) * 512],
                    mkcb(), bank(0)[0:1, :], bank(1)[0:1, :], bank(2), bank(3),
                    ONES, R0, R1, R2r, RB2,
                    lambda c: ARH[:, RS_SQ + c * 512:RS_SQ + (c + 1) * 512])

            vpk = lambda ts: ARH[:, RS_SQ + ts * H65:RS_SQ + (ts + 1) * H65]
            p.W("ve", "te")
            p.OP("ve", lambda E: E.memset(
                ARH[:, RS_SQ:RS_SQ + 4 * H65].rearrange(
                    "p (t h e) -> p (t h) e", h=HL, e=65)[:, :, 64:65], 1.0),
                sem="ve")

            kph = lambda ts: ARH[:, RS_XB + ts * WKV:RS_XB + (ts + 1) * WKV]
            evdone = {}
            for ts in range(4):
                for kvs in range(2):
                    it = ts * 2 + kvs
                    b = it % 2
                    if it >= 2:
                        p.W("te", "ve", evdone[it - 2])
                    if it == 0:
                        p.W("te", "ve")
                    for c in range(DC):
                        p.OP("te", lambda E, ts=ts, kvs=kvs, c=c, b=b, xnc=xnc: E.matmul(
                            ps[:, b * 512:b * 512 + WKV],
                            xnc(c)[:, ts * 128:(ts + 1) * 128],
                            ARH[:, AW_KV + c * D + kvs * WKV:
                                AW_KV + c * D + (kvs + 1) * WKV],
                            start=(c == 0), stop=(c == DC - 1)),
                            sem="te" if c == DC - 1 else None)
                    p.W("ve", "te")
                    p.W("sc", "te")
                    if kvs == 0:
                        p.OP("ve", lambda E, b=b: E.tensor_tensor(
                            TMPA[:, 0:WKV], ps[:, b * 512:b * 512 + WKV],
                            BKV[:, 0:WKV], ALU.add))
                        p.OP("ve", lambda E: E.tensor_scalar_min(
                            TMPB[:, 0:WKV], TMPA[:, 0:WKV], 0.0), sem="ve")
                        p.W("sc", "ve")
                        p.OP("sc", lambda E: E.activation(
                            TMPB[:, 0:WKV], TMPB[:, 0:WKV], AF.Exp), sem="sc")
                        p.W("ve", "sc")
                        p.OP("ve", lambda E, ts=ts, kph=kph: E.scalar_tensor_tensor(
                            kph(ts), TMPA[:, 0:WKV], 0.0, TMPB[:, 0:WKV],
                            ALU.max, ALU.add), sem="ve")
                    else:
                        p.OP("ve", lambda E, ts=ts, b=b, vpk=vpk: E.tensor_tensor(
                            vpk(ts).rearrange("p (h e) -> p h e", e=65)
                            [:, :, 0:64],
                            ps[:, b * 512:b * 512 + WKV].rearrange(
                                "p (h e) -> p h e", e=64),
                            BKV[:, WKV:2 * WKV].rearrange("p (h e) -> p h e", e=64),
                            ALU.add),
                            sem="ve")
                    evdone[it] = p.cnt["ve"]
            p.W("te", "ve")
            for h in range(HL):
                bb, off = 4 + h // 4, (h % 4) * 65
                for ts in range(4):
                    p.OP("te", lambda E, ts=ts, h=h, bb=bb, off=off, kph=kph, vpk=vpk: E.matmul(
                        ps[0:64, bb * 512 + off:bb * 512 + off + 65],
                        kph(ts)[:, h * 64:(h + 1) * 64],
                        vpk(ts)[:, h * 65:(h + 1) * 65],
                        start=(ts == 0), stop=(ts == 3)),
                        sem="te" if (ts == 3 and h == HL - 1) else None)
            p.W("ve", "te")
            for bb in range(4, 4 + (HL + 3) // 4):
                w4 = min(4, HL - (bb - 4) * 4) * 65
                co = (bb - 4) * 4 * 65
                if tt == 0:
                    p.OP("ve", lambda E, bb=bb, w4=w4, co=co: E.tensor_copy(
                        KVA[0:64, co:co + w4],
                        ps[0:64, bb * 512:bb * 512 + w4]))
                else:
                    p.OP("ve", lambda E, bb=bb, w4=w4, co=co: E.tensor_tensor(
                        KVA[0:64, co:co + w4], KVA[0:64, co:co + w4],
                        ps[0:64, bb * 512:bb * 512 + w4], ALU.add))

        p.OP("ve", lambda E: E.tensor_copy(KVS2[0:64, :], KVA[0:64, :]))
        p.W("gp", "ve")
        p.DMA("gp", KVS2[64:128, :], KVS2[0:64, :], "gps")
        p.W("te", "gps")

        if stop_after == "A2":
            p.W("sy", "gps")
            p.emit(block)
            return nc
        qph = lambda c: ARH[:, RS_Q4 + c * 512:RS_Q4 + (c + 1) * 512]
        xnt = lambda c: ARH[:, RS_XNS + c * 512:RS_XNS + (c + 1) * 512]
        att0 = lambda s: ARH[:, ATT0 + s * 512:ATT0 + (s + 1) * 512]

        for at in range(NTS):
            p.W("sy", "ve")
            p.W("sy", "syo")
            p.DMA("sy", SCRF[:, XF:XF + DC * 512].rearrange("p (c w) -> p c w", w=512),
                  _fm3(x0Tg[:, at * 512:(at + 1) * 512]), "syl")
            p.W("gp", "te")
            p.W("gp", "ve")
            p.DMA("gp", ARH[:, RS_SQ:RS_SQ + DC * 512].rearrange("p (c w) -> p c w", w=512),
                  _fm3(x0Tg[:, at * 512:(at + 1) * 512]), "gpl")
            p.W("ve", "syl")
            p.W("sc", "gpl")
            p.W("te", "gpl")

            def cb_xn(c, rb, mb):
                p.OP("ve", lambda E, c=c, rb=rb: E.tensor_tensor(
                    TMPA, SCRF[:, XF + c * 512:XF + (c + 1) * 512], rb, ALU.mult))
                p.OP("ve", lambda E, c=c, mb=mb: E.tensor_tensor(
                    xnt(c), TMPA, mb, ALU.subtract),
                    sem="ve" if c == DC - 1 else None)

            emit_ln(p, cfg, 512,
                    lambda c: ARH[:, RS_SQ + c * 512:RS_SQ + (c + 1) * 512],
                    cb_xn, bank(0)[0:1, :], bank(1)[0:1, :], bank(2), bank(3),
                    ONES, R0, R1, R2r, RB2,
                    lambda c: ARH[:, RS_XB + c * 512:RS_XB + (c + 1) * 512])

            evq = {}
            for qc in range(DCq):
                b = qc % 2
                if qc >= 2:
                    p.W("te", "ve", evq[qc - 2])
                else:
                    p.W("te", "ve")
                for c in range(DC):
                    p.OP("te", lambda E, qc=qc, c=c, b=b: E.matmul(
                        bank(b),
                        ARH[:, AW_Q + c * WKV + qc * 128:AW_Q + c * WKV + (qc + 1) * 128],
                        xnt(c), start=(c == 0), stop=(c == DC - 1)),
                        sem="te" if c == DC - 1 else None)
                p.W("ve", "te")
                p.OP("ve", lambda E, b=b, qc=qc: E.tensor_scalar(
                    TMPB, bank(b), bcol(OBQ, qc), 0.0, ALU.add, ALU.min), sem="ve")
                p.W("sc", "ve")
                p.OP("sc", lambda E: E.activation(TMPB, TMPB, AF.Exp), sem="sc")
                p.OP("ve", lambda E, b=b, qc=qc: E.tensor_scalar(
                    TMPA, bank(b), bcol(OBQ, qc), 0.0, ALU.add, ALU.max))
                p.W("ve", "sc")
                p.OP("ve", lambda E, qc=qc: E.tensor_tensor(
                    qph(qc), TMPA, TMPB, ALU.add), sem="ve")
                evq[qc] = p.cnt["ve"]

            for hl in range(HL):
                B = (hl % 2) * 64
                b4 = 4 + hl % 2
                slot = hl % 2
                p.W("te", "ve")
                p.W("te", "sc")
                p.OP("te", lambda E, hl=hl, B=B, b4=b4: E.matmul(
                    ps[0:65, b4 * 512:b4 * 512 + 512],
                    KVS2[B:B + 64, hl * 65:(hl + 1) * 65],
                    qph(hl // 2)[B:B + 64, :],
                    start=True, stop=True), sem="te")
                p.W("ve", "te")
                p.OP("ve", lambda E, b4=b4: E.tensor_scalar_max(
                    SCRF[64:65, TAc:TAc + 512],
                    ps[64:65, b4 * 512:b4 * 512 + 512], 1e-6))
                p.OP("ve", lambda E: E.reciprocal(
                    SCRF[64:65, TAc:TAc + 512], SCRF[64:65, TAc:TAc + 512]))
                p.OP("ve", lambda E: E.tensor_copy(
                    TMPB.bitcast(BF16)[64:65, 0:512], SCRF[64:65, TAc:TAc + 512]),
                    sem="ve")
                p.W("te", "ve")
                p.OP("te", lambda E: E.matmul(
                    ps[0:128, 6 * 512:7 * 512], ONES[64:65, :],
                    TMPB.bitcast(BF16)[64:65, 0:512], start=True, stop=True),
                    sem="te")
                p.W("sc", "te")
                p.OP("sc", lambda E, b4=b4: E.activation(
                    TMPA.bitcast(BF16)[0:64, 0:512],
                    ps[0:64, b4 * 512:b4 * 512 + 512], AF.Copy), sem="sc")
                p.W("ve", "sc")
                p.W("ve", "gps")
                p.OP("ve", lambda E, slot=slot: E.tensor_tensor(
                    att0(slot)[0:64, :].rearrange("p (a b) -> p b a", a=JR),
                    TMPA.bitcast(BF16)[0:64, 0:512],
                    ps[0:64, 6 * 512:6 * 512 + 512], ALU.mult), sem="ve")
                p.W("gp", "ve")
                dst = bass.AP(tensor=scr_d.tensor,
                              offset=hl * RPH + at * MJ,
                              ap=[[T, 64], [64 * T, JR], [1, MJ]])
                p.DMA("gp", dst,
                      att0(slot)[0:64, :].rearrange("p (a b) -> p a b", a=JR),
                      "gps")

        if stop_after == "A4a":
            p.W("sy", "gps")
            p.emit(block)
            return nc

        att = lambda c: ARH[:, RS_SQ + c * 512:RS_SQ + (c + 1) * 512]
        prj = lambda c: ARH[:, RS_XNS + c * 512:RS_XNS + (c + 1) * 512]
        mo = qph
        SQ4 = att

        for tt in range(NT):
            p.W("sy", "ve")
            p.W("sy", "syo")
            p.DMA("sy", SCRF[:, XF:XF + DC * 512].rearrange("p (c w) -> p c w", w=512),
                  _fm3(x0T[:, tt * 512:(tt + 1) * 512]), "syl")
            p.W("gp", "te")
            p.W("gp", "ve")
            p.W("gp", "gps")
            p.DMA("gp", ARH[:, RS_SQ:RS_SQ + DC * 512].rearrange("p (c w) -> p c w", w=512),
                  _fm3(scr_d[:, tt * 512:(tt + 1) * 512]), "gps")
            p.W("ve", "syl")
            p.W("te", "gps")

            evo = {}
            for oc in range(DC):
                b = oc % 2
                if oc >= 2:
                    p.W("te", "sc", evo[oc - 2])
                else:
                    p.W("te", "ve")
                for c in range(DC):
                    p.OP("te", lambda E, oc=oc, c=c, b=b: E.matmul(
                        bank(b),
                        ARH[:, AW_P + c * D + oc * 128:AW_P + c * D + (oc + 1) * 128],
                        att(c), start=(c == 0), stop=(c == DC - 1)),
                        sem="te" if c == DC - 1 else None)
                p.W("sc", "te")
                p.OP("sc", lambda E, oc=oc, b=b: E.activation(
                    prj(oc), bank(b), AF.Identity, bias=bcol(OBP, oc)), sem="sc")
                evo[oc] = p.cnt["sc"]

            def cb_inner(c, rb, mb, tt=tt):
                p.OP("ve", lambda E, c=c, rb=rb: E.tensor_tensor(
                    TMPA, prj(c), rb, ALU.mult))
                p.OP("ve", lambda E, c=c, mb=mb: E.tensor_tensor(
                    TMPA, TMPA, mb, ALU.subtract))
                p.OP("ve", lambda E, c=c: E.tensor_scalar(
                    TMPA, TMPA, bcol(OGA, c), bcol(OBE, c), ALU.mult, ALU.add))
                p.OP("ve", lambda E, c=c: E.tensor_tensor(
                    mo(c), TMPA, AR1[:, c * T + tt * 512:c * T + tt * 512 + 512],
                    ALU.add), sem="ve" if c == DC - 1 else None)

            emit_ln(p, cfg, 512, lambda c: prj(c), cb_inner,
                    bank(0)[0:1, :], bank(1)[0:1, :], bank(2), bank(3),
                    ONES, R0, R1, R2r, RB2, lambda c: SQ4(c))

            def cb_outer(c, rb, mb):
                p.OP("ve", lambda E, c=c, rb=rb: E.tensor_tensor(
                    TMPA, mo(c), rb, ALU.mult))
                p.OP("ve", lambda E, c=c, mb=mb: E.tensor_tensor(
                    TMPA, TMPA, mb, ALU.subtract))
                p.OP("ve", lambda E, c=c: E.tensor_tensor(
                    SCRF[:, XF + c * 512:XF + (c + 1) * 512],
                    TMPA, SCRF[:, XF + c * 512:XF + (c + 1) * 512], ALU.add))
                p.OP("ve", lambda E, c=c: E.tensor_copy(
                    prj(c), SCRF[:, XF + c * 512:XF + (c + 1) * 512]),
                    sem="ve" if c == DC - 1 else None)

            emit_ln(p, cfg, 512, lambda c: mo(c), cb_outer,
                    bank(0)[0:1, :], bank(1)[0:1, :], bank(2), bank(3),
                    ONES, R0, R1, R2r, RB2, lambda c: SQ4(c))

            p.W("sy", "ve")
            p.DMA("sy", xsp[:, tt * 512:(tt + 1) * 512].rearrange("(c p) w -> p c w", p=128),
                  SCRF[:, XF:XF + DC * 512].rearrange("p (c w) -> p c w", w=512), "syo")

            def cb_ffn(c, rb, mb, tt=tt):
                p.OP("ve", lambda E, c=c, rb=rb: E.tensor_tensor(
                    TMPA, prj(c), rb, ALU.mult))
                p.OP("ve", lambda E, c=c, mb=mb: E.tensor_tensor(
                    AR1[:, c * T + tt * 512:c * T + tt * 512 + 512],
                    TMPA, mb, ALU.subtract), sem="ve" if c == DC - 1 else None)

            emit_ln(p, cfg, 512, lambda c: prj(c), cb_ffn,
                    bank(0)[0:1, :], bank(1)[0:1, :], bank(2), bank(3),
                    ONES, R0, R1, R2r, RB2, lambda c: SQ4(c))

        if stop_after == "A4":
            p.W("sy", "syo")
            p.emit(block)
            return nc
        emit_ffn_stages(p, cfg, ARH, AR1, WBUF, SCRF, ps, bcol, OS1, OS2,
                        OSH1, OSH2, w1T, w2T, h1T, h2T, g_sp, g2_sp)

        p.W("gp", "te")
        p.W("gp", "sps")
        p.DMA("gp", AR1[:, 0:HC * 512].rearrange("p (c w) -> p c w", w=512),
              g2_sp[0].rearrange("c p w -> p c w"), "gps")
        p.DMA("gp", ARH[:, W3_0:W3_0 + HC * D].rearrange("p (c w) -> p c w", w=D),
              w3T.rearrange("(c p) w -> p c w", p=128), "gpl")
        p.DMA("gp", ARH[:, WK_0:WK_0 + DC * D].rearrange("p (c w) -> p c w", w=D),
              _fm3(wkT), "gw0")
        p.DMA("gp", ARH[:, WV_0:WV_0 + DC * D].rearrange("p (c w) -> p c w", w=D),
              _fm3(wvT), "gw0")
        BV1 = SCRF[:, TBc:TBc + 512].bitcast(BF16)
        p.W("te", "ve")
        p.W("te", "sc")
        nslv = D // 512
        for j in range(nslv):
            p.OP("te", lambda E, j=j: E.matmul(
                bank(j % 2), ONES[0:1, :], BVR[0:1, j * 512:(j + 1) * 512],
                start=True, stop=True), sem="te" if j == nslv - 1 else None)
        p.W("ve", "te")
        for j in range(nslv):
            p.OP("ve", lambda E, j=j: E.tensor_copy(
                BV1[:, j * 512:(j + 1) * 512], bank(j % 2)),
                sem="ve" if j == nslv - 1 else None)
        if dbg:
            p.W("sy", "ve")
            p.DMA("sy", dbgV_o, BV1, "syo")

        ybf = lambda oc: WBUF[:, oc * 512:(oc + 1) * 512]
        xn2 = lambda c: WBUF[:, c * 512:(c + 1) * 512]
        x2b = lambda c: ARH[:, X2B_0 + c * 512:X2B_0 + (c + 1) * 512]
        kbf = lambda kc: ARH[:, KB_0 + kc * 512:KB_0 + (kc + 1) * 512]
        sq9 = lambda c: ARH[:, SQ9_0 + c * 512:SQ9_0 + (c + 1) * 512]

        for ts in range(NT):
            if ts > 0:
                p.W("gp", "te")
                p.DMA("gp", AR1[:, 0:HC * 512].rearrange("p (c w) -> p c w", w=512),
                      g2_sp[ts].rearrange("c p w -> p c w"), "gps")
            if ts == 0:
                p.W("sy", "te")
                p.W("sy", "ve")
            p.W("sy", "syo")
            p.DMA("sy", SCRF[:, XF:XF + DC * 512].rearrange("p (c w) -> p c w", w=512),
                  xsp[:, ts * 512:(ts + 1) * 512].rearrange("(c p) w -> p c w", p=128),
                  "syl")
            p.W("te", "gps")
            p.W("te", "syl")
            if ts == 0:
                p.W("te", "gpl")
            p.W("te", "ve")
            p.W("te", "sc")
            for oc in range(DC):
                for c in range(HC):
                    p.OP("te", lambda E, oc=oc, c=c: E.matmul(
                        bank(oc),
                        ARH[:, W3_0 + c * D + oc * 128:W3_0 + c * D + (oc + 1) * 128],
                        AR1[:, c * 512:(c + 1) * 512],
                        start=(c == 0), stop=(c == HC - 1)),
                        sem="te" if (c == HC - 1 and oc == DC - 1) else None)
            p.W("sc", "te")
            for oc in range(DC):
                p.OP("sc", lambda E, oc=oc: E.activation(
                    ybf(oc), bank(oc), AF.Copy, scale=bcol(OS3, oc)),
                    sem="sc" if oc == DC - 1 else None)

            def cb_x2(c, rb, mb):
                p.OP("ve", lambda E, c=c, rb=rb: E.tensor_tensor(
                    TMPA, ybf(c), rb, ALU.mult))
                p.OP("ve", lambda E, c=c, mb=mb: E.tensor_tensor(
                    TMPA, TMPA, mb, ALU.subtract))
                p.OP("ve", lambda E, c=c: E.tensor_tensor(
                    SCRF[:, XF + c * 512:XF + (c + 1) * 512],
                    TMPA, SCRF[:, XF + c * 512:XF + (c + 1) * 512], ALU.add))
                p.OP("ve", lambda E, c=c: E.tensor_copy(
                    x2b(c), SCRF[:, XF + c * 512:XF + (c + 1) * 512]),
                    sem="ve" if c == DC - 1 else None)

            emit_ln(p, cfg, 512, lambda c: ybf(c), cb_x2,
                    bank(0)[0:1, :], bank(1)[0:1, :], bank(2), bank(3),
                    ONES, R0, R1, R2r, RB2, sq9)

            p.W("sy", "ve")
            p.DMA("sy", x2T_o[:, ts * 512:(ts + 1) * 512].rearrange("(c p) w -> p c w", p=128),
                  SCRF[:, XF:XF + DC * 512].rearrange("p (c w) -> p c w", w=512), "syo")

            def cb_xn2(c, rb, mb):
                p.OP("ve", lambda E, c=c, rb=rb: E.tensor_tensor(
                    TMPA, x2b(c), rb, ALU.mult))
                p.OP("ve", lambda E, c=c, mb=mb: E.tensor_tensor(
                    xn2(c), TMPA, mb, ALU.subtract),
                    sem="ve" if c == DC - 1 else None)

            emit_ln(p, cfg, 512, lambda c: x2b(c), cb_xn2,
                    bank(0)[0:1, :], bank(1)[0:1, :], bank(2), bank(3),
                    ONES, R0, R1, R2r, RB2, sq9)
            p.W("sy", "ve")
            p.DMA("sy", xn2T_o[:, ts * 512:(ts + 1) * 512].rearrange("(c p) w -> p c w", p=128),
                  WBUF[:, 0:DC * 512].rearrange("p (c w) -> p c w", w=512),
                  "syo")

            if ts == 0:
                p.W("te", "gw0")
            evk = {}
            for kc in range(DC):
                b = kc % 2
                if kc >= 2:
                    p.W("te", "sc", evk[kc - 2])
                else:
                    p.W("te", "ve")
                for c in range(DC):
                    p.OP("te", lambda E, kc=kc, c=c, b=b: E.matmul(
                        bank(b),
                        ARH[:, WK_0 + c * D + kc * 128:WK_0 + c * D + (kc + 1) * 128],
                        xn2(c), start=(c == 0), stop=(c == DC - 1)),
                        sem="te" if c == DC - 1 else None)
                p.W("sc", "te")
                p.OP("sc", lambda E, kc=kc, b=b: E.activation(
                    kbf(kc), bank(b), AF.Identity, bias=bcol(OBK, kc)), sem="sc")
                evk[kc] = p.cnt["sc"]
            p.W("sy", "sc")
            p.DMA("sy", KT_o[:, ts * 512:(ts + 1) * 512].rearrange("(c p) w -> p c w", p=128),
                  ARH[:, KB_0:KB_0 + DC * 512].rearrange("p (c w) -> p c w", w=512),
                  "syo")

            evv = {}
            for sub in range(4):
                for vs in range(NQ):
                    it = sub * NQ + vs
                    b = 2 + it % 2
                    if it >= 2:
                        p.W("te", "ve", evv[it - 2])
                    for c in range(DC):
                        p.OP("te", lambda E, sub=sub, vs=vs, c=c, b=b: E.matmul(
                            bank(b),
                            xn2(c)[:, sub * 128:(sub + 1) * 128],
                            ARH[:, WV_0 + c * D + vs * 512:WV_0 + c * D + (vs + 1) * 512],
                            start=(c == 0), stop=(c == DC - 1)),
                            sem="te" if c == DC - 1 else None)
                    p.W("ve", "te")
                    p.OP("ve", lambda E, b=b, vs=vs: E.tensor_tensor(
                        ARH[:, VS_0:VS_0 + 512], bank(b),
                        BV1[:, vs * 512:(vs + 1) * 512],
                        ALU.add), sem="ve")
                    evv[it] = p.cnt["ve"]
                    p.W("sy", "ve")
                    p.DMA("sy", V_o[ts * 512 + sub * 128:ts * 512 + (sub + 1) * 128,
                                    vs * 512:(vs + 1) * 512],
                          ARH[:, VS_0:VS_0 + 512], "syo")
                    p.W("ve", "syo")

        if dbg:
            p.W("sy", "ve")
            p.W("sy", "sc")
            p.DMA("sy", dbgV2_o, BV1, "syo")
        p.W("sy", "syo")
        p.W("sy", "gps")
        p.emit(block)
    return nc



def build_launch_B(cfg, stop_after=None, dbg=False):
    D, T, S, DK, H, DH = cfg.D, cfg.T, cfg.S, cfg.DK, cfg.H, cfg.DH
    DC, HC, NT, NTS = cfg.DC, cfg.HC, cfg.NT, cfg.NTS
    NQS = T // 128
    NKC = S // 128
    NKG = S // 1024
    P2 = T + 384
    UCX = 256 + T
    nc = bass.Bass("TRN2", target_bir_lowering=False)

    din = lambda n, s, d=F32: nc.dram_tensor(n, s, d, kind="ExternalInput").ap()
    dout = lambda n, s, d: nc.dram_tensor(n, s, d, kind="ExternalOutput").ap()

    x2T = din("x2T", [D, T])
    xn2T = din("xn2T", [D, T], BF16)
    KTp = din("KTp", [D, S], BF16)
    Vp = din("Vp", [S, D], BF16)
    erxT = din("erxT", [64, T + 1])
    wqT = din("wqT", [D, D])
    bqF = din("bqF", [128, DC])
    woT = din("woT", [D, D])
    boF = din("boF", [128, DC])
    idn = din("idn", [128, 128], BF16)
    fw1T = din("fw1T", [D, DH])
    fw2T = din("fw2T", [D, DH])
    fh1T = din("fh1T", [DH, DH])
    fh2T = din("fh2T", [DH, DH])
    fw3T = din("fw3T", [DH, D])
    fs1F = din("fs1F", [128, HC])
    fs2F = din("fs2F", [128, HC])
    fsh1F = din("fsh1F", [128, HC])
    fsh2F = din("fsh2F", [128, HC])
    fs3F = din("fs3F", [128, DC])

    x4T_o = dout("x4T", [D, T], F32)
    dbgX3_o = dout("dbgX3", [D, T], F32) if dbg else None
    dbgAtt_o = dout("dbgAtt", [D, 512], BF16) if dbg else None

    scr_d = nc.dram_tensor("scr_d", [D, T], BF16, kind="Internal").ap()
    xsp = nc.dram_tensor("xsp", [D, T], F32, kind="Internal").ap()
    g_sp = nc.dram_tensor("g_sp", [HC, 128, T], BF16, kind="Internal").ap()
    g2_sp = nc.dram_tensor("g2_sp", [NT, HC, 128, 512], BF16, kind="Internal").ap()
    r2d = nc.dram_tensor("r2d", [H, T, P2], BF16, kind="Internal")

    BW_Q = 0
    BV_P = BW_Q + DC * D
    BR_D = BV_P + NKC * 130
    RWID0 = sum(min(T - s * 128, T) for s in range(4))
    BE_X = BR_D + 2 * (RWID0 + 512)
    BK_S = BE_X + 4096
    BA_T = BK_S + S
    BSC = BA_T + DC * 512
    B_END = BSC + 3 * DC * 512
    B9_END = HC * D + 3 * DC * 512 + 512
    ARH_COLS = max(HC * T, B_END, B9_END, DC * (256 + 256) + 64)
    AR1_COLS = max(DC * T, HC * 512, 2 * T)

    with ExitStack() as ctx:
        sbt = lambda n, s, d: ctx.enter_context(nc.sbuf_tensor(n, s, d))
        AR1 = sbt("AR1", [128, AR1_COLS], BF16)
        ARH = sbt("ARH", [128, ARH_COLS], BF16)
        WBUF = sbt("WBUF", [128, max(2 * DC * 256, HC * 128, DC * 512)], BF16)
        SCRF = sbt("SCRF", [128, DC * 512 + 512 * 2 + 1536], F32)
        UCR2 = sbt("UCR2", [128, UCX], BF16)
        IDN = sbt("IDN", [128, 128], BF16)
        ONES = sbt("ONES", [128, 128], BF16)
        BIAS = sbt("BIAS", [128, 16 * DC + 4 * HC], F32)
        RB2 = sbt("RB2", [1, 1024], BF16)
        ERX = ARH.ap().bitcast(F32)[0:64, (BA_T // 2):(BA_T // 2) + T + 1] \
            if False else None
        ps = ctx.enter_context(nc.psum_tensor("ps", [128, 4096], F32))
        block = ctx.enter_context(nc.Block())
        p = P(nc, ctx)
        ERX = ARH[:, BA_T:BA_T + 2 * (T + 1) + 2].bitcast(F32)[0:64, 0:T + 1]
        for s in ("syl", "syo", "gpl", "gps", "te", "ve", "sc",
                  "gw0", "gw1", "gw2", "scl", "sps"):
            p.sem(s)

        bank = lambda i: ps[:, i * 512:(i + 1) * 512]
        XF = 0
        TAc = DC * 512
        TBc = TAc + 512
        RWc = TBc + 512
        R0 = SCRF[0:1, RWc:RWc + 512]
        R1 = SCRF[0:1, RWc + 512:RWc + 1024]
        R2r = SCRF[0:1, RWc + 1024:RWc + 1536]
        TMPA = SCRF[:, TAc:TAc + 512]
        TMPB = SCRF[:, TBc:TBc + 512]

        OBQ, OBO, OS3 = 0, DC, 4 * DC
        OS1 = 16 * DC
        OS2, OSH1, OSH2 = OS1 + HC, OS1 + 2 * HC, OS1 + 3 * HC
        bcol = lambda off, i: BIAS[:, off + i:off + i + 1]

        for dst, srcp in ((BIAS[:, OBQ:OBQ + DC], bqF),
                          (BIAS[:, OBO:OBO + DC], boF),
                          (BIAS[:, OS3:OS3 + DC], fs3F),
                          (BIAS[:, OS1:OS1 + HC], fs1F),
                          (BIAS[:, OS2:OS2 + HC], fs2F),
                          (BIAS[:, OSH1:OSH1 + HC], fsh1F),
                          (BIAS[:, OSH2:OSH2 + HC], fsh2F)):
            p.DMA("sy", dst, srcp, "syl")
        p.DMA("sy", ERX[:, :], erxT, "syl")
        p.DMA("sy", IDN[:, :], idn, "syl")
        p.DMA("gp", ARH[:, BW_Q:BW_Q + DC * D].rearrange("p (c w) -> p c w", w=D),
              _fm3(wqT), "gpl")
        p.OP("ve", lambda E: E.memset(ONES[:, :], 1.0))
        p.OP("ve", lambda E: E.memset(UCR2[:, 0:256], 0.0))
        p.W("ve", "syl")
        p.OP("ve", lambda E: E.tensor_scalar(
            UCR2[0:64, 256:256 + T], ERX[0:64, 0:T], ERX[0:64, T:T + 1], None,
            ALU.subtract))
        p.W("gp", "ve")
        p.DMA("gp", UCR2[64:128, :], UCR2[0:64, :], "gps")

        p.W("te", "gpl")
        p.W("te", "gps")
        xntmp = lambda c: ARH[:, BR_D + c * 512:BR_D + (c + 1) * 512]
        for tt in range(NT):
            p.W("sy", "ve")
            p.W("sy", "sc")
            p.DMA("sy", ARH[:, BR_D:BR_D + DC * 512].rearrange("p (c w) -> p c w", w=512),
                  _fm3(xn2T[:, tt * 512:(tt + 1) * 512]), "syl")
            p.W("te", "syl")
            evq = {}
            for qc in range(DC):
                b = qc % 2
                if qc >= 2:
                    p.W("te", "sc", evq[qc - 2])
                for c in range(DC):
                    p.OP("te", lambda E, qc=qc, c=c, b=b: E.matmul(
                        bank(b),
                        ARH[:, BW_Q + c * D + qc * 128:BW_Q + c * D + (qc + 1) * 128],
                        xntmp(c), start=(c == 0), stop=(c == DC - 1)),
                        sem="te" if c == DC - 1 else None)
                p.W("sc", "te")
                p.OP("sc", lambda E, qc=qc, tt=tt, b=b: E.activation(
                    AR1[:, qc * T + tt * 512:qc * T + (tt + 1) * 512],
                    bank(b), AF.Identity, bias=bcol(OBQ, qc)))
                evq[qc] = p.cnt["sc"]

        for h in range(H):
            B = (h % 2) * 64
            c_h = h // 2
            for t128 in range(NQS):
                lq0 = t128 * 128
                Wt = T + 256 - lq0
                offs, widths, o = [], [], 0
                while o < Wt:
                    w = min(512, Wt - o)
                    offs.append(o); widths.append(w); o += w
                for i, (o0, w) in enumerate(zip(offs, widths)):
                    b = i % 2
                    p.W("te", "sc")
                    p.OP("te", lambda E, h=h, B=B, c_h=c_h, lq0=lq0, o0=o0, w=w, b=b:
                         E.matmul(ps[0:128, b * 512:b * 512 + w],
                                  AR1[B:B + 64, c_h * T + lq0:c_h * T + lq0 + 128],
                                  UCR2[B:B + 64, o0:o0 + w],
                                  start=True, stop=True), sem="te")
                    p.W("sc", "te")
                    p.W("sc", "gps")
                    p.OP("sc", lambda E, b=b, w=w: E.activation(
                        TMPA.bitcast(BF16)[:, 0:w] if b == 0 else
                        TMPB.bitcast(BF16)[:, 0:w],
                        ps[0:128, b * 512:b * 512 + w], AF.Copy))
                    p.W("gp", "sc")
                    diag = bass.AP(tensor=r2d.ap().tensor,
                                   offset=h * T * P2 + lq0 * (P2 + 1) + o0,
                                   ap=[[P2 + 1, 128], [1, w]])
                    p.DMA("gp", diag,
                          TMPA.bitcast(BF16)[:, 0:w] if b == 0 else
                          TMPB.bitcast(BF16)[:, 0:w], "gps")
        if stop_after == "B3":
            p.W("sy", "gps")
            p.emit(block)
            return nc

        p.W("gp", "te")
        p.DMA("gp", ARH[:, BW_Q:BW_Q + DC * D].rearrange("p (c w) -> p c w", w=D),
              _fm3(woT), "gpl")

        vpk = lambda kc, hh: ARH[:, BV_P + kc * 130 + hh * 65:BV_P + kc * 130 + (hh + 1) * 65]
        att = lambda c: ARH[:, BA_T + c * 512:BA_T + (c + 1) * 512]
        prj = lambda c: ARH[:, BSC + c * 512:BSC + (c + 1) * 512]
        sqb = lambda c: ARH[:, BSC + DC * 512 + c * 512:BSC + DC * 512 + (c + 1) * 512]
        x3b = lambda c: ARH[:, BSC + 2 * DC * 512 + c * 512:BSC + 2 * DC * 512 + (c + 1) * 512]

        for qt in range(NT):
            for pr in range(H // 2):
                h, h2 = 2 * pr, 2 * pr + 1
                p.W("sy", "te")
                p.W("sy", "sc")
                p.DMA("sy", ARH[:, BK_S:BK_S + S],
                      KTp[pr * 128:(pr + 1) * 128, :], "syl")
                p.W("gp", "te")
                p.W("gp", "ve")
                for hh in range(2):
                    p.DMA("gp", ARH[:, BV_P:BV_P + NKC * 130].rearrange(
                        "p (kc e) -> p kc e", e=130)[:, :, hh * 65:hh * 65 + 64],
                        Vp[:, (2 * pr + hh) * 64:(2 * pr + hh) * 64 + 64].rearrange(
                            "(kc p) e -> p kc e", p=128), "gpl")
                p.OP("ve", lambda E: E.memset(ARH[:, BV_P:BV_P + NKC * 130].rearrange(
                    "p (kc e) -> p kc e", e=65)[:, :, 64:65], 1.0))
                rect_off = {}
                ro = 0
                p.W("gp", "gps")
                for hh in range(2):
                    for s4 in range(4):
                        lq0 = qt * 512 + s4 * 128
                        wid = T - lq0
                        if wid <= 0:
                            continue
                        rect_off[(hh, s4)] = ro
                        p.W("gp", "te")
                        p.DMA("gp", ARH[:, BR_D + ro:BR_D + ro + wid],
                              r2d.ap()[2 * pr + hh, lq0:lq0 + 128,
                                       lq0 + 256:lq0 + 256 + wid], "gps")
                        ro += wid
                p.W("te", "syl")
                p.W("te", "gpl")
                p.W("te", "gps")
                p.W("te", "ve")
                for kg in range(S // 256):
                    for kci in range(2):
                        kc = kg * 2 + kci
                        band_subs = []
                        for s4 in range(4):
                            lq0 = qt * 512 + s4 * 128
                            if kc * 128 >= lq0 + T and (0, s4) in rect_off:
                                band_subs.append(s4)
                        for hh, b0 in ((0, 0), (1, 2)):
                            B = hh * 64
                            last_te = (hh == 1 and kci == 1 and not band_subs)
                            p.OP("te", lambda E, B=B, kc=kc, kci=kci, b0=b0, pr=pr, qt=qt, bs=bool(band_subs):
                                 E.matmul(ps[0:128, (b0 + kci) * 512:(b0 + kci + 1) * 512],
                                          ARH[B:B + 64, BK_S + kc * 128:BK_S + (kc + 1) * 128],
                                          AR1[B:B + 64, pr * T + qt * 512:pr * T + (qt + 1) * 512],
                                          start=True, stop=not bs),
                                 sem="te" if last_te else None)
                        for hh in range(2):
                            b0 = hh * 2
                            for si, s4 in enumerate(band_subs):
                                lq0 = qt * 512 + s4 * 128
                                kcr = (kc * 128 - (lq0 + T)) // 128
                                ro = rect_off[(hh, s4)]
                                lastb = (si == len(band_subs) - 1)
                                last_te = (hh == 1 and kci == 1 and lastb)
                                p.OP("te", lambda E, s4=s4, kci=kci, b0=b0, ro=ro, kcr=kcr, lastb=lastb:
                                     E.matmul(ps[0:128, (b0 + kci) * 512 + s4 * 128:
                                                 (b0 + kci) * 512 + (s4 + 1) * 128],
                                              ARH[:, BR_D + ro + kcr * 128:BR_D + ro + (kcr + 1) * 128],
                                              IDN[:, :], start=False, stop=lastb),
                                     sem="te" if last_te else None)
                    for hh, b0 in ((0, 0), (1, 2)):
                        p.W("sc", "te")
                        p.OP("sc", lambda E, hh=hh, b0=b0, kg=kg: E.activation(
                            ARH[:, BE_X + hh * 2048 + (kg % 2) * 1024:
                                BE_X + hh * 2048 + (kg % 2) * 1024 + 1024],
                            ps[0:128, b0 * 512:(b0 + 2) * 512], AF.Exp, scale=0.125))
                    p.W("te", "sc")
                    for kci in range(2):
                        kc = kg * 2 + kci
                        for hh in range(2):
                            p.OP("te", lambda E, hh=hh, kc=kc, kci=kci, kg=kg: E.matmul(
                                ps[0:65, (4 + hh) * 512:(4 + hh) * 512 + 512],
                                vpk(kc, hh),
                                ARH[:, BE_X + hh * 2048 + (kg % 2) * 1024 + kci * 512:
                                    BE_X + hh * 2048 + (kg % 2) * 1024 + (kci + 1) * 512],
                                start=(kc == 0), stop=(kc == NKC - 1)),
                                sem="te" if (kc == NKC - 1 and hh == 1) else None)
                for hh in range(2):
                    B = hh * 64
                    bv4 = 4 + hh
                    p.W("ve", "te")
                    p.OP("ve", lambda E, bv4=bv4: E.reciprocal(
                        SCRF[64:65, TAc:TAc + 512], ps[64:65, bv4 * 512:bv4 * 512 + 512]))
                    p.OP("ve", lambda E: E.tensor_copy(
                        TMPB.bitcast(BF16)[64:65, 0:512], SCRF[64:65, TAc:TAc + 512]))
                    p.W("te", "ve")
                    p.OP("te", lambda E: E.matmul(
                        ps[0:128, 6 * 512:7 * 512], ONES[64:65, :],
                        TMPB.bitcast(BF16)[64:65, 0:512], start=True, stop=True),
                        sem="te")
                    p.W("sc", "te")
                    p.OP("sc", lambda E, bv4=bv4, B=B: E.activation(
                        TMPA.bitcast(BF16)[0:64, 0:512],
                        ps[0:64, bv4 * 512:bv4 * 512 + 512], AF.Copy))
                    if hh == 0:
                        p.W("ve", "sc")
                        p.OP("ve", lambda E, pr=pr: E.tensor_tensor(
                            att(pr)[0:64, :], TMPA.bitcast(BF16)[0:64, 0:512],
                            ps[0:64, 6 * 512:6 * 512 + 512], ALU.mult))
                    else:
                        p.W("gp", "sc")
                        p.DMA("gp", TMPB.bitcast(BF16)[64:128, 512:1024],
                              TMPA.bitcast(BF16)[0:64, 0:512], "gps")
                        p.W("ve", "gps")
                        p.OP("ve", lambda E, pr=pr: E.tensor_tensor(
                            att(pr)[64:128, :], TMPB.bitcast(BF16)[64:128, 512:1024],
                            ps[64:128, 6 * 512:7 * 512], ALU.mult))

            if dbg and qt == 0:
                p.W("sy", "ve")
                p.DMA("sy", dbgAtt_o.rearrange("(c p) w -> p c w", p=128),
                      ARH[:, BA_T:BA_T + DC * 512].rearrange("p (c w) -> p c w", w=512),
                      "syo")
            p.W("sy", "ve")
            p.W("sy", "syo")
            p.DMA("sy", SCRF[:, XF:XF + DC * 512].rearrange("p (c w) -> p c w", w=512),
                  _fm3(x2T[:, qt * 512:(qt + 1) * 512]), "syl")
            p.W("ve", "syl")
            evo = {}
            for oc in range(DC):
                b = oc % 2
                p.W("te", "ve")
                if oc >= 2:
                    p.W("te", "sc", evo[oc - 2])
                for c in range(DC):
                    p.OP("te", lambda E, oc=oc, c=c, b=b: E.matmul(
                        bank(b),
                        ARH[:, BW_Q + c * D + oc * 128:BW_Q + c * D + (oc + 1) * 128],
                        att(c), start=(c == 0), stop=(c == DC - 1)),
                        sem="te" if c == DC - 1 else None)
                p.W("sc", "te")
                p.OP("sc", lambda E, oc=oc, b=b: E.activation(
                    prj(oc), bank(b), AF.Identity, bias=bcol(OBO, oc)))
                evo[oc] = p.cnt["sc"]

            def cb_x3(c, rb, mb):
                p.OP("ve", lambda E, c=c, rb=rb: E.tensor_tensor(
                    TMPA, prj(c), rb, ALU.mult))
                p.OP("ve", lambda E, c=c, mb=mb: E.tensor_tensor(
                    TMPA, TMPA, mb, ALU.subtract))
                p.OP("ve", lambda E, c=c: E.tensor_tensor(
                    SCRF[:, XF + c * 512:XF + (c + 1) * 512],
                    TMPA, SCRF[:, XF + c * 512:XF + (c + 1) * 512], ALU.add))
                p.OP("ve", lambda E, c=c: E.tensor_copy(
                    x3b(c), SCRF[:, XF + c * 512:XF + (c + 1) * 512]))

            emit_ln(p, cfg, 512, lambda c: prj(c), cb_x3,
                    bank(0)[0:1, :], bank(1)[0:1, :], bank(2), bank(3),
                    ONES, R0, R1, R2r, RB2, lambda c: sqb(c))
            p.W("sy", "ve")
            p.DMA("sy", xsp[:, qt * 512:(qt + 1) * 512].rearrange("(c p) w -> p c w", p=128),
                  SCRF[:, XF:XF + DC * 512].rearrange("p (c w) -> p c w", w=512), "syo")

            def cb_xn3(c, rb, mb, qt=qt):
                p.OP("ve", lambda E, c=c, rb=rb: E.tensor_tensor(
                    TMPA, x3b(c), rb, ALU.mult))
                p.OP("ve", lambda E, c=c, mb=mb, qt=qt: E.tensor_tensor(
                    AR1[:, c * T + qt * 512:c * T + (qt + 1) * 512],
                    TMPA, mb, ALU.subtract))

            emit_ln(p, cfg, 512, lambda c: x3b(c), cb_xn3,
                    bank(0)[0:1, :], bank(1)[0:1, :], bank(2), bank(3),
                    ONES, R0, R1, R2r, RB2, lambda c: sqb(c))

        if stop_after == "B4":
            p.W("sy", "syo")
            p.emit(block)
            return nc

        emit_ffn_stages(p, cfg, ARH, AR1, WBUF, SCRF, ps, bcol, OS1, OS2,
                        OSH1, OSH2, fw1T, fw2T, fh1T, fh2T, g_sp, g2_sp)

        W3_0 = 0
        B9S = HC * D
        ybf = lambda oc: WBUF[:, oc * 512:(oc + 1) * 512]
        x4b = lambda c: ARH[:, B9S + c * 512:B9S + (c + 1) * 512]
        sq9 = lambda c: ARH[:, B9S + DC * 512 + c * 512:B9S + DC * 512 + (c + 1) * 512]
        p.W("gp", "te")
        p.W("gp", "sps")
        p.DMA("gp", AR1[:, 0:HC * 512].rearrange("p (c w) -> p c w", w=512),
              g2_sp[0].rearrange("c p w -> p c w"), "gps")
        p.DMA("gp", ARH[:, W3_0:W3_0 + HC * D].rearrange("p (c w) -> p c w", w=D),
              fw3T.rearrange("(c p) w -> p c w", p=128), "gpl")
        for ts in range(NT):
            if ts > 0:
                p.W("gp", "te")
                p.DMA("gp", AR1[:, 0:HC * 512].rearrange("p (c w) -> p c w", w=512),
                      g2_sp[ts].rearrange("c p w -> p c w"), "gps")
            if ts == 0:
                p.W("sy", "te")
                p.W("sy", "ve")
            p.W("sy", "syo")
            p.DMA("sy", SCRF[:, XF:XF + DC * 512].rearrange("p (c w) -> p c w", w=512),
                  xsp[:, ts * 512:(ts + 1) * 512].rearrange("(c p) w -> p c w", p=128),
                  "syl")
            p.W("te", "gps")
            p.W("te", "syl")
            if ts == 0:
                p.W("te", "gpl")
            p.W("te", "ve")
            p.W("te", "sc")
            for oc in range(DC):
                for c in range(HC):
                    p.OP("te", lambda E, oc=oc, c=c: E.matmul(
                        bank(oc),
                        ARH[:, W3_0 + c * D + oc * 128:W3_0 + c * D + (oc + 1) * 128],
                        AR1[:, c * 512:(c + 1) * 512],
                        start=(c == 0), stop=(c == HC - 1)),
                        sem="te" if (c == HC - 1 and oc == DC - 1) else None)
            p.W("sc", "te")
            for oc in range(DC):
                p.OP("sc", lambda E, oc=oc: E.activation(
                    ybf(oc), bank(oc), AF.Copy, scale=bcol(OS3, oc)))

            def cb_x4(c, rb, mb):
                p.OP("ve", lambda E, c=c, rb=rb: E.tensor_tensor(
                    TMPA, ybf(c), rb, ALU.mult))
                p.OP("ve", lambda E, c=c, mb=mb: E.tensor_tensor(
                    TMPA, TMPA, mb, ALU.subtract))
                p.OP("ve", lambda E, c=c: E.tensor_tensor(
                    SCRF[:, XF + c * 512:XF + (c + 1) * 512],
                    TMPA, SCRF[:, XF + c * 512:XF + (c + 1) * 512], ALU.add))

            emit_ln(p, cfg, 512, lambda c: ybf(c), cb_x4,
                    bank(0)[0:1, :], bank(1)[0:1, :], bank(2), bank(3),
                    ONES, R0, R1, R2r, RB2, sq9)
            p.W("sy", "ve")
            p.DMA("sy", x4T_o[:, ts * 512:(ts + 1) * 512].rearrange("(c p) w -> p c w", p=128),
                  SCRF[:, XF:XF + DC * 512].rearrange("p (c w) -> p c w", w=512), "syo")

        if dbg:
            p.W("sy", "syo")
            p.DMA("sy", dbgX3_o, xsp, "syo")
        p.W("sy", "syo")
        p.W("sy", "gps")
        p.emit(block)
    return nc


def _fm_vec(v, nch):
    return np.ascontiguousarray(np.asarray(v, np.float32).reshape(nch, 128).T)


def _t(a):
    return np.ascontiguousarray(np.asarray(a, np.float32).T)


def kernel(**inp):
    cfg = Cfg()
    D, T, S, DC, HC, H = cfg.D, cfg.T, cfg.S, cfg.DC, cfg.HC, cfg.H
    ML = T

    WKV = D // 2
    DCq = WKV // 128
    Wqkv = np.asarray(inp["L0_attn_W_qkv"], np.float32)
    bqkv = np.asarray(inp["L0_attn_b_qkv"], np.float32)
    sharedA = {
        "wprojT": _t(inp["L0_attn_W_proj"]),
        "bprojF": _fm_vec(inp["L0_attn_b_proj"], DC),
        "gammaF": _fm_vec(inp["L0_attn_gamma"], DC),
        "betaF": _fm_vec(inp["L0_attn_beta"], DC),
        "w1T": _t(inp["L0_ffn_w1_w"]),
        "w2T": _t(inp["L0_ffn_w2_w"]),
        "h1T": _t(inp["L0_ffn_h1_w"]),
        "h2T": _t(inp["L0_ffn_h2_w"]),
        "w3T": _t(inp["L0_ffn_w3_w"]),
        "s1F": _fm_vec(inp["L0_ffn_w1_s"], HC),
        "s2F": _fm_vec(inp["L0_ffn_w2_s"], HC),
        "sh1F": _fm_vec(inp["L0_ffn_h1_s"], HC),
        "sh2F": _fm_vec(inp["L0_ffn_h2_s"], HC),
        "s3F": _fm_vec(inp["L0_ffn_w3_s"], DC),
        "wkT": _t(inp["L1_attn_Wk"]),
        "wvT": _t(inp["L1_attn_Wv"]),
        "bkF": _fm_vec(inp["L1_attn_bk"], DC),
        "bvrow": np.ascontiguousarray(
            np.asarray(inp["L1_attn_bv"], np.float32)).reshape(1, D),
    }
    x = np.asarray(inp["L0_attn_H"], np.float32)
    hgA = []
    for hf in range(2):
        hgA.append({
            "wqgT": np.ascontiguousarray(Wqkv[WKV * hf:WKV * (hf + 1)].T),
            "wkvT": np.ascontiguousarray(np.concatenate(
                [Wqkv[D + WKV * hf:D + WKV * (hf + 1)],
                 Wqkv[2 * D + WKV * hf:2 * D + WKV * (hf + 1)]], 0).T),
            "bqF": _fm_vec(bqkv[WKV * hf:WKV * (hf + 1)], DCq),
            "bkvrow": np.ascontiguousarray(np.concatenate(
                [bqkv[D + WKV * hf:D + WKV * (hf + 1)],
                 bqkv[2 * D + WKV * hf:2 * D + WKV * (hf + 1)]])).reshape(1, D),
        })
    mapsA = []
    for c in range(8):
        b, hf = c // 2, c % 2
        own = x[b, hf * T:(hf + 1) * T]
        sib = x[b, (1 - hf) * T:(2 - hf) * T]
        m = {"x0T": np.ascontiguousarray(np.concatenate([own, sib], 0).T),
             "x0Tg": np.ascontiguousarray(x[b].T)}
        m.update(sharedA)
        m.update(hgA[hf])
        mapsA.append(m)

    ncA = build_launch_A(cfg)
    resA = run_bass_kernel_spmd(ncA, mapsA, core_ids=list(range(8))).results

    Er = np.asarray(inp["L1_attn_Er"], np.float32)
    sharedB = {
        "wqT": _t(inp["L1_attn_Wq"]),
        "bqF": _fm_vec(inp["L1_attn_bq"], DC),
        "woT": _t(inp["L1_attn_Wo"]),
        "boF": _fm_vec(inp["L1_attn_bo"], DC),
        "idn": np.eye(128, dtype=ml_dtypes.bfloat16),
        "fw1T": _t(inp["L1_ffn_w1_w"]),
        "fw2T": _t(inp["L1_ffn_w2_w"]),
        "fh1T": _t(inp["L1_ffn_h1_w"]),
        "fh2T": _t(inp["L1_ffn_h2_w"]),
        "fw3T": _t(inp["L1_ffn_w3_w"]),
        "fs1F": _fm_vec(inp["L1_ffn_w1_s"], HC),
        "fs2F": _fm_vec(inp["L1_ffn_w2_s"], HC),
        "fsh1F": _fm_vec(inp["L1_ffn_h1_s"], HC),
        "fsh2F": _fm_vec(inp["L1_ffn_h2_s"], HC),
        "fs3F": _fm_vec(inp["L1_ffn_w3_s"], DC),
    }
    erxs = []
    for hf in range(2):
        o = hf * T
        erx = np.empty((ML + 1, 64), np.float32)
        for cc in range(ML):
            erx[cc] = Er[min(ML - 1 - cc + o, ML - 1)]
        erx[ML] = Er[ML - 1]
        erxs.append(np.ascontiguousarray(erx.T))

    mapsB = []
    for c in range(8):
        b, hf = c // 2, c % 2
        sibc = c ^ 1
        m = {
            "x2T": resA[c]["x2T"],
            "xn2T": resA[c]["xn2T"],
            "KTp": np.ascontiguousarray(
                np.concatenate([resA[c]["KT"], resA[sibc]["KT"]], axis=1)),
            "Vp": np.ascontiguousarray(
                np.concatenate([resA[c]["V"], resA[sibc]["V"]], axis=0)),
            "erxT": erxs[hf],
        }
        m.update(sharedB)
        mapsB.append(m)

    ncB = build_launch_B(cfg)
    resB = run_bass_kernel_spmd(ncB, mapsB, core_ids=list(range(8))).results

    out = np.empty((4, S, D), np.float32)
    for c in range(8):
        b, hf = c // 2, c % 2
        out[b, hf * T:(hf + 1) * T] = resB[c]["x4T"].T
    return out
